# version 09
# baseline (speedup 1.0000x reference)
import contextlib

import numpy as np

import concourse.bass as bass
import concourse.mybir as mybir
from concourse.bass_utils import run_bass_kernel_spmd

F32 = mybir.dt.float32
BF16 = mybir.dt.bfloat16
ALU = mybir.AluOpType
ACT = mybir.ActivationFunctionType
AXX = mybir.AxisListType.X

LN_EPS = 1e-5
LN2 = 0.6931471805599453


class Cfg:
    def __init__(self, D=2048, T=2048, TT=512, S=64, I=8192, debug=False):
        self.D, self.T, self.TT, self.S, self.I = D, T, TT, S, I
        self.NK = D // 128
        self.TO = T // 2
        self.NTT = T // TT
        self.NTC = self.TO // 128
        self.NI = I // 128
        self.NQ = self.NI // 4
        self.JM = 6 * D
        self.NJT = self.JM // TT
        self.NJ2 = self.JM // 128
        self.debug = debug


class B:

    ENGINES = ("sync", "tensor", "vector", "scalar", "gpsimd")

    def __init__(self, nc):
        self.nc = nc
        self.sems = {}
        self.cnt = {}
        self.waited = {}
        self.phase = []

    def make_sems(self, stack, names):
        for n in names:
            self.sems[n] = stack.enter_context(self.nc.semaphore(n))
            self.cnt[n] = 0

    def need(self, engine, deps):
        for s, v in deps:
            if v is None or v <= 0:
                continue
            key = (engine, s)
            if self.waited.get(key, 0) >= v:
                continue
            self.waited[key] = v
            sem = self.sems[s]
            self.phase.append((engine, lambda e, sem=sem, v=v: e.wait_ge(sem, v)))

    def op(self, engine, fn, deps=(), inc=None, inc_by=1, then_by=None):
        deps = list(deps)
        if engine == "vector" and self.cnt.get("V", 0) > 0:
            deps.append(("V", self.cnt["V"]))
        if engine == "scalar" and self.cnt.get("SC", 0) > 0:
            deps.append(("SC", self.cnt["SC"]))
        if engine == "gpsimd" and self.cnt.get("G", 0) > 0:
            deps.append(("G", self.cnt["G"]))
        self.need(engine, deps)
        if inc is None:
            self.phase.append((engine, fn))
            return None
        sem = self.sems[inc]
        self.cnt[inc] += inc_by
        val = self.cnt[inc]
        tn = inc_by if then_by is None else then_by
        self.phase.append(
            (engine, lambda e, fn=fn, sem=sem, n=tn: fn(e).then_inc(sem, n)))
        return val

    def run_phase(self, name):
        sched, self.phase = self.phase, []
        if not sched:
            return
        with self.nc.Block(name) as block:
            for eng in self.ENGINES:
                items = [f for e, f in sched if e == eng]
                if items:
                    getattr(block, eng)(
                        lambda engine, items=items: [f(engine) for f in items])


def build_program(c: Cfg):
    nc = bass.Bass("TRN2", target_bir_lowering=False)
    D, T, TT, S, NK, TO, NTT, NTC = c.D, c.T, c.TT, c.S, c.NK, c.TO, c.NTT, c.NTC

    xT = nc.dram_tensor("xT", [D, T], F32, kind="ExternalInput").ap()
    x_tm_h = nc.dram_tensor("x_tm", [TO, D], F32, kind="ExternalInput").ap()
    wsh_pk = nc.dram_tensor("wsh_pk", [NK * 128, NK * 128], F32, kind="ExternalInput").ap()
    wk_pk = nc.dram_tensor("wk_pk", [128, NK * S], F32, kind="ExternalInput").ap()
    wv_pk = nc.dram_tensor("wv_pk", [128, NK * S], F32, kind="ExternalInput").ap()
    woutT = nc.dram_tensor("woutT", [S, D], F32, kind="ExternalInput").ap()
    gw_r = nc.dram_tensor("gw_r", [c.I, D], F32, kind="ExternalInput").ap()
    up_r = nc.dram_tensor("up_r", [c.I, D], F32, kind="ExternalInput").ap()
    dw_r = nc.dram_tensor("dw_r", [D, c.I], F32, kind="ExternalInput").ap()
    gwT_h = nc.dram_tensor("gwT", [D, c.I], F32, kind="ExternalInput").ap()
    upT_h = nc.dram_tensor("upT", [D, c.I], F32, kind="ExternalInput").ap()
    dwT_h = nc.dram_tensor("dwT", [c.I, D], F32, kind="ExternalInput").ap()
    wmod_h = nc.dram_tensor("wmod", [D, c.JM], F32, kind="ExternalInput").ap()
    gatepp = nc.dram_tensor("gatepp", [128, NK], F32, kind="ExternalInput").ap()
    lngpp = nc.dram_tensor("lngpp", [128, NK], F32, kind="ExternalInput").ap()
    lnbpp = nc.dram_tensor("lnbpp", [128, NK], F32, kind="ExternalInput").ap()
    cpp = nc.dram_tensor("cpp", [128, NK], F32, kind="ExternalInput").ap()
    td_h = nc.dram_tensor("td", [S, 1], F32, kind="ExternalInput").ap()
    tf_h = nc.dram_tensor("tf", [S, 1], F32, kind="ExternalInput").ap()
    cmask_h = nc.dram_tensor("cmask", [S, 1], F32, kind="ExternalInput").ap()
    out_tm_h = nc.dram_tensor("out_tm", [TO, D], F32, kind="ExternalOutput").ap()
    dbg = {}
    if c.debug:
        for nm, shp, dt in [
            ("dbg_hb", [128, NK * T], BF16),
            ("dbg_states", [S, TO], F32),
            ("dbg_xmid", [128, NTC * D], F32),
            ("dbg_x2f", [128, NTC * D], BF16),
            ("dbg_h0", [128, TT], BF16),
            ("dbg_h2t0", [128, 2048], F32),
            ("dbg_dqx0", [128, 2048], BF16),
            ("dbg_hq0", [128, TT], BF16),
        ]:
            dbg[nm] = nc.dram_tensor(nm, shp, dt, kind="ExternalOutput").ap()

    b = B(nc)
    st_top = contextlib.ExitStack()
    st_mix = contextlib.ExitStack()
    st_p4 = contextlib.ExitStack()
    st_hb = contextlib.ExitStack()
    st_xs = contextlib.ExitStack()
    st_xm = contextlib.ExitStack()
    with st_top:
        b.make_sems(st_top, ["V", "SC", "G", "P", "Q", "GA", "GB", "GC", "GD",
                             "HX", "HY", "HO", "HO0", "HO1", "HLW", "HLD",
                             "RL0", "RL1", "FL0", "FL1", "TS0", "TS1",
                             "TX0", "TX1", "TH0", "TH1"])
        sb = lambda st, name, shp, dt, side=None: st.enter_context(
            nc.sbuf_tensor(name, shp, dt, side=side) if side else
            nc.sbuf_tensor(name, shp, dt))
        ps = [st_top.enter_context(nc.psum_tensor(f"ps{i}", [128, TT], F32))
              for i in range(8)]
        bank_free = [("Q", 0)] * 8
        bank_ready = [0] * 8

        g_sig = sb(st_top, "g_sig", [128, NK], F32)
        one_m_g = sb(st_top, "one_m_g", [128, NK], F32)
        lng = sb(st_top, "lng", [128, NK], F32)
        lnb = sb(st_top, "lnb", [128, NK], F32)
        u_neg = sb(st_top, "u_neg", [S, 1], F32)
        w_dec = sb(st_top, "w_dec", [S, 1], F32)
        cmask = sb(st_top, "cmask_s", [S, 1], F32)
        modpp = sb(st_top, "modpp", [128, 6 * NK], F32)
        ca1 = sb(st_top, "ca1", [128, NK], F32)
        ones_bf = sb(st_top, "ones_bf", [128, 1], BF16)
        qguard = sb(st_top, "qguard", [128, 1], F32)
        ones16 = sb(st_top, "ones16", [128, 1], F32)
        onesr = sb(st_top, "onesr", [1, 128], F32)
        ga_tm = sb(st_p4, "ga_tm", [128, D], BF16, side="right")
        sb_tm = sb(st_p4, "sb_tm", [128, D], BF16, side="right")
        cb1_tm = sb(st_p4, "cb1_tm", [128, D], F32, side="right")
        gb_tm = sb(st_top, "gb_tm", [128, D], BF16)
        stat = sb(st_mix, "stat", [1, 6 * TT], F32, side="right")
        MU, M2, MSQ, VAR, SD, RINV = range(6)
        stv = lambda i: stat[0:1, i * TT:(i + 1) * TT]
        sq_scr = [sb(st_mix, f"sq{i}", [128, TT], BF16, side="right") for i in range(2)]
        f_scr = [sb(st_mix, f"fs{i}", [128, TT], F32, side="right") for i in range(4)]
        states_bf = sb(st_mix, "states_bf", [S, TO], BF16, side="right")

        with contextlib.ExitStack() as ph:
            gpv = sb(ph, "gpv", [128, NK], F32)
            tdv = sb(ph, "tdv", [S, 1], F32)
            tfv = sb(ph, "tfv", [S, 1], F32)
            cvv = sb(ph, "cvv", [128, NK], F32)
            silu_c = sb(ph, "silu_c", [128, NK], F32)
            silu_cb = sb(ph, "silu_cb", [128, NK], BF16)
            diag = sb(ph, "diag", [128, NK * NK], BF16)
            iden = sb(ph, "iden", [128, 128], F32)
            wmbuf = [sb(ph, f"wmbuf{i}", [128, 8 * TT], BF16) for i in range(2)]
            stackb = sb(ph, "stackb", [NK, c.JM], F32)
            mrow = sb(ph, "mrow", [1, c.JM], F32)
            mtmp = sb(ph, "mtmp", [c.NJ2, 128], F32)

            for dst, src in [(gpv, gatepp), (tdv, td_h), (tfv, tf_h), (cvv, cpp),
                             (cmask, cmask_h), (lng, lngpp), (lnb, lnbpp)]:
                b.op("gpsimd", lambda e, d=dst, s=src: e.dma_start(out=d[:, :], in_=s[:, :]),
                     inc="GC", inc_by=16)
            v_gc = b.cnt["GC"]
            for t_, v_ in [(ones_bf, 1.0), (ones16, 1.0), (onesr, 1.0),
                           (qguard, 1e-35)]:
                b.op("gpsimd", lambda e, t=t_, v=v_: e.memset(t[:, :], v), inc="Q")
            b.op("gpsimd", lambda e: e.memset(diag[:, :], 0.0), inc="Q")
            b.op("gpsimd", lambda e: e.iota(iden[:, :], [[1, 128]], channel_multiplier=-1,
                                            allow_small_or_imprecise_dtypes=True), inc="Q")
            v_q = b.cnt["Q"]
            v_id = b.op("vector", lambda e: e.tensor_scalar(
                out=iden[:, :], in0=iden[:, :], scalar1=0.0, scalar2=None,
                op0=ALU.is_equal), deps=[("Q", v_q)], inc="V")
            v_sig = b.op("scalar", lambda e: e.activation(out=g_sig[:, :], in_=gpv[:, :],
                         func=ACT.Sigmoid), deps=[("GC", v_gc)], inc="SC")
            b.op("vector", lambda e: e.tensor_scalar(
                out=one_m_g[:, :], in0=g_sig[:, :], scalar1=-1.0, scalar2=1.0,
                op0=ALU.mult, op1=ALU.add), deps=[("SC", v_sig)], inc="V")
            v_u = b.op("scalar", lambda e: e.activation(out=u_neg[:, :], in_=tfv[:, :],
                       func=ACT.Exp), deps=[("GC", v_gc)], inc="SC")
            b.op("vector", lambda e: e.tensor_scalar(
                out=u_neg[:, :], in0=u_neg[:, :], scalar1=-1.0, scalar2=None,
                op0=ALU.mult), deps=[("SC", v_u)], inc="V")
            b.op("scalar", lambda e: e.activation(out=w_dec[:, :], in_=tdv[:, :],
                 func=ACT.Exp), deps=[("GC", v_gc)], inc="SC")
            v_silu = b.op("scalar", lambda e: e.activation(out=silu_c[:, :], in_=cvv[:, :],
                          func=ACT.Silu), deps=[("GC", v_gc)], inc="SC")
            v_scb = b.op("vector", lambda e: e.tensor_copy(out=silu_cb[:, :], in_=silu_c[:, :]),
                         deps=[("SC", v_silu)], inc="V")
            for k in range(NK):
                b.op("vector", lambda e, k=k: e.tensor_copy(
                    out=diag[:, k * NK + k:k * NK + k + 1], in_=silu_cb[:, k:k + 1]),
                    deps=[("V", v_scb), ("Q", v_q)], inc="V")
            v_diag = b.cnt["V"]

            gen_done_P = {}
            kgen = 0
            j0 = 0
            while j0 < c.NJT:
                wave = min(8, c.NJT - j0)
                for k in range(NK):
                    buf = wmbuf[kgen % 2]
                    gsem = "GA" if kgen % 2 == 0 else "GB"
                    vdma = b.op("gpsimd", lambda e, k=k, j0=j0, w=wave, buf=buf:
                                e.dma_start(out=buf[:, 0:w * TT],
                                            in_=wmod_h[k * 128:(k + 1) * 128,
                                                       j0 * TT:(j0 + w) * TT]),
                                deps=[("P", gen_done_P.get(kgen % 2, 0))],
                                inc=gsem, inc_by=16)
                    mmdeps = [(gsem, vdma), ("V", v_diag)]
                    if k == 0:
                        for j in range(wave):
                            mmdeps.append(bank_free[j])
                    vmm = None
                    for j in range(wave):
                        vmm = b.op("tensor", lambda e, k=k, j=j, j0=j0, buf=buf: e.matmul(
                            ps[j][0:NK, 0:TT], diag[:, k * NK:(k + 1) * NK],
                            buf[:, j * TT:(j + 1) * TT],
                            start=(k == 0), stop=(k == NK - 1)),
                            deps=mmdeps, inc="P")
                        mmdeps = []
                    gen_done_P[kgen % 2] = vmm
                    kgen += 1
                for j in range(wave):
                    bank_ready[j] = b.cnt["P"]
                    vev = b.op("vector", lambda e, j=j, j0=j0: e.tensor_copy(
                        out=stackb[0:NK, (j0 + j) * TT:(j0 + j + 1) * TT],
                        in_=ps[j][0:NK, 0:TT]),
                        deps=[("P", bank_ready[j])], inc="V")
                    bank_free[j] = ("V", vev)
                j0 += wave
            for j in range(c.NJT):
                bank = 6 if j % 2 == 0 else 7
                vmm = b.op("tensor", lambda e, j=j, bank=bank: e.matmul(
                    ps[bank][0:1, 0:TT], ones16[0:NK, 0:1],
                    stackb[0:NK, j * TT:(j + 1) * TT], start=True, stop=True),
                    deps=[bank_free[bank], ("V", b.cnt["V"])], inc="P")
                bank_ready[bank] = vmm
                vev = b.op("vector", lambda e, j=j, bank=bank: e.tensor_copy(
                    out=mrow[0:1, j * TT:(j + 1) * TT], in_=ps[bank][0:1, 0:TT]),
                    deps=[("P", vmm)], inc="V")
                bank_free[bank] = ("V", vev)
            v_mrow = b.cnt["V"]
            v_rel = b.op("gpsimd", lambda e: e.dma_start(
                out=mtmp[:, :],
                in_=mrow[0:1, :].rearrange("p (a x) -> p a x", x=128)),
                deps=[("V", v_mrow)], inc="GC", inc_by=16)
            v_tr = b.op("tensor", lambda e: e.transpose(
                ps[0][:, 0:c.NJ2], mtmp[:, :], iden[0:c.NJ2, 0:c.NJ2]),
                deps=[("GC", v_rel), ("V", v_id), bank_free[0]], inc="P")
            bank_ready[0] = v_tr
            v_mpp = b.op("scalar", lambda e: e.copy(out=modpp[:, :], in_=ps[0][:, 0:c.NJ2]),
                         deps=[("P", v_tr)], inc="SC")
            bank_free[0] = ("SC", v_mpp)
            b.op("vector", lambda e: e.tensor_scalar(
                out=ca1[:, :], in0=modpp[:, NK:2 * NK], scalar1=1.0, scalar2=None,
                op0=ALU.add), deps=[("SC", v_mpp)], inc="V")

            def emit_tm_bcast(slot, dst, mode, bankpair):
                for j in range(D // TT):
                    bank = bankpair[j % 2]
                    vmm = b.op("tensor", lambda e, j=j, slot=slot, bank=bank: e.matmul(
                        ps[bank][:, 0:TT], onesr[:, :],
                        mrow[0:1, slot * D + j * TT:slot * D + (j + 1) * TT],
                        start=True, stop=True),
                        deps=[("V", v_mrow), bank_free[bank]], inc="P")
                    bank_ready[bank] = vmm
                    dsl = lambda dst=dst, j=j: dst[:, j * TT:(j + 1) * TT]
                    if mode == "signmax":
                        scr = f_scr[j % 2]
                        scr2 = f_scr[2 + (j % 2)]
                        vge = b.op("vector", lambda e, bank=bank, scr=scr: e.tensor_scalar(
                            out=scr[:, :], in0=ps[bank][:, 0:TT], scalar1=0.0,
                            scalar2=None, op0=ALU.is_ge), deps=[("P", vmm)], inc="V")
                        vsg = b.op("vector", lambda e, scr=scr: e.tensor_scalar(
                            out=scr[:, :], in0=scr[:, :], scalar1=2.0, scalar2=-1.0,
                            op0=ALU.mult, op1=ALU.add), inc="V")
                        vab0 = b.op("scalar", lambda e, bank=bank, scr2=scr2:
                                    e.activation(out=scr2[:, :], in_=ps[bank][:, 0:TT],
                                                 func=ACT.Abs),
                                    deps=[("P", vmm), ("V", b.cnt["V"])], inc="SC")
                        vab = b.op("vector", lambda e, scr2=scr2: e.tensor_scalar(
                            out=scr2[:, :], in0=scr2[:, :], scalar1=0.25,
                            scalar2=None, op0=ALU.max), deps=[("SC", vab0)], inc="V")
                        vf = b.op("vector", lambda e, scr=scr, scr2=scr2, dsl=dsl:
                                  e.tensor_tensor(out=dsl(), in0=scr[:, :],
                                                  in1=scr2[:, :], op=ALU.mult), inc="V")
                        bank_free[bank] = ("V", vf)
                    elif mode == "add1":
                        vf = b.op("vector", lambda e, bank=bank, dsl=dsl: e.tensor_scalar(
                            out=dsl(), in0=ps[bank][:, 0:TT], scalar1=1.0,
                            scalar2=None, op0=ALU.add), deps=[("P", vmm)], inc="V")
                        bank_free[bank] = ("V", vf)
                    else:
                        vf = b.op("vector", lambda e, bank=bank, dsl=dsl: e.tensor_copy(
                            out=dsl(), in_=ps[bank][:, 0:TT]), deps=[("P", vmm)], inc="V")
                        bank_free[bank] = ("V", vf)

            emit_tm_bcast(2, ga_tm, "signmax", (0, 1))
            emit_tm_bcast(5, gb_tm, "signmax", (2, 3))
            emit_tm_bcast(4, cb1_tm, "add1", (4, 5))
            emit_tm_bcast(3, sb_tm, "copy", (6, 7))
            b.run_phase("p0_mod")

        def ln_stats_and_bcast(get_xbf, extra_dep, banks, n_chunks=NK):
            bSX, bSX2, bBM, bBR = banks
            sqdep = [0, 0]
            dep0 = list(extra_dep)
            for k in range(n_chunks):
                scr = sq_scr[k % 2]
                vsq = b.op("scalar", lambda e, k=k, scr=scr: e.activation(
                    out=scr[:, :], in_=get_xbf(k), func=ACT.Square),
                    deps=dep0 + [("P", sqdep[k % 2])], inc="SC")
                mmdeps = dep0 + [("SC", vsq)]
                if k == 0:
                    mmdeps += [bank_free[bSX], bank_free[bSX2]]
                b.op("tensor", lambda e, k=k, first=(k == 0), last=(k == n_chunks - 1):
                     e.matmul(ps[bSX][0:1, 0:TT], ones_bf[:, :], get_xbf(k),
                              start=first, stop=last), deps=mmdeps, inc="P")
                vmm2 = b.op("tensor", lambda e, k=k, scr=scr,
                            first=(k == 0), last=(k == n_chunks - 1):
                            e.matmul(ps[bSX2][0:1, 0:TT], ones_bf[:, :], scr[:, :],
                                     start=first, stop=last), inc="P")
                sqdep[k % 2] = vmm2
            bank_ready[bSX] = bank_ready[bSX2] = b.cnt["P"]
            pst = b.cnt["P"]
            v1 = b.op("vector", lambda e: e.tensor_scalar(
                out=stv(MU), in0=ps[bSX][0:1, 0:TT], scalar1=1.0 / D, scalar2=None,
                op0=ALU.mult), deps=[("P", pst)], inc="V")
            v2 = b.op("vector", lambda e: e.tensor_scalar(
                out=stv(M2), in0=ps[bSX2][0:1, 0:TT], scalar1=1.0 / D, scalar2=None,
                op0=ALU.mult), deps=[("P", pst)], inc="V")
            bank_free[bSX] = ("V", v1)
            bank_free[bSX2] = ("V", v2)
            v3 = b.op("vector", lambda e: e.tensor_tensor(
                out=stv(MSQ), in0=stv(MU), in1=stv(MU), op=ALU.mult),
                deps=[("V", v1)], inc="V")
            v4 = b.op("vector", lambda e: e.scalar_tensor_tensor(
                out=stv(VAR), in0=stv(M2), scalar=LN_EPS, in1=stv(MSQ),
                op0=ALU.add, op1=ALU.subtract), deps=[("V", v3)], inc="V")
            v5 = b.op("scalar", lambda e: e.activation(
                out=stv(SD), in_=stv(VAR), func=ACT.Sqrt),
                deps=[("V", v4)], inc="SC")
            v6 = b.op("vector", lambda e: e.reciprocal(out=stv(RINV), in_=stv(SD)),
                      deps=[("SC", v5)], inc="V")
            vbm = b.op("tensor", lambda e: e.matmul(
                ps[bBM][:, 0:TT], onesr[:, :], stv(MU), start=True, stop=True),
                deps=[("V", v1), bank_free[bBM]], inc="P")
            vbr = b.op("tensor", lambda e: e.matmul(
                ps[bBR][:, 0:TT], onesr[:, :], stv(RINV), start=True, stop=True),
                deps=[("V", v6), bank_free[bBR]], inc="P")
            bank_ready[bBM], bank_ready[bBR] = vbm, vbr
            return bBM, bBR, vbm, vbr

        def ln_apply(x_ap, out_ap, bBM, bBR, pdep, sc_pp, bias_pp, scr):
            va = b.op("vector", lambda e: e.tensor_tensor(
                out=scr[:, :], in0=x_ap, in1=ps[bBM][:, 0:TT], op=ALU.subtract),
                deps=pdep, inc="V")
            vb = b.op("vector", lambda e: e.tensor_tensor(
                out=scr[:, :], in0=scr[:, :], in1=ps[bBR][:, 0:TT], op=ALU.mult),
                deps=[("V", va)], inc="V")
            vc = b.op("vector", lambda e: e.tensor_scalar(
                out=out_ap, in0=scr[:, :], scalar1=sc_pp, scalar2=bias_pp,
                op0=ALU.mult, op1=ALU.add), deps=[("V", vb)], inc="V")
            return vc

        hb = sb(st_hb, "hb", [128, NK * T], BF16)
        with contextlib.ExitStack() as ph:
            xb = sb(ph, "xb", [128, NK * T], BF16)
            for k in range(NK):
                b.op("gpsimd", lambda e, k=k: e.dma_start(
                    out=xb[:, k * T:(k + 1) * T], in_=xT[k * 128:(k + 1) * 128, :]),
                    inc="GC", inc_by=16)
            v_xb = b.cnt["GC"]
            for ti in range(NTT):
                banks = (0, 1, 2, 3) if ti % 2 == 0 else (4, 5, 6, 7)
                getx = lambda k, ti=ti: xb[:, k * T + ti * TT:k * T + ti * TT + TT]
                bBM, bBR, vbm, vbr = ln_stats_and_bcast(getx, [("GC", v_xb)], banks)
                lastap = 0
                for k in range(NK):
                    lastap = ln_apply(
                        getx(k), hb[:, k * T + ti * TT:k * T + ti * TT + TT],
                        bBM, bBR, [("P", max(vbm, vbr))],
                        ca1[:, k:k + 1], modpp[:, 0 * NK + k:0 * NK + k + 1],
                        f_scr[k % 2])
                bank_free[bBM] = bank_free[bBR] = ("V", lastap)
            if c.debug:
                vv = b.cnt["V"]
                b.op("sync", lambda e: e.dma_start(out=dbg["dbg_hb"][:, :], in_=hb[:, :]),
                     deps=[("V", vv)], inc="HO", inc_by=16)
                b.need("sync", [("HO", b.cnt["HO"])])
            b.run_phase("p1_h")

        x_shift = sb(st_xs, "x_shift", [128, NK * T], BF16, side="right")
        with contextlib.ExitStack() as ph:
            wsb = [sb(ph, f"wsb{i}", [128, NK * 128], BF16) for i in range(2)]
            gen_done = {0: 0, 1: 0}
            for e_i in range(NK):
                buf = wsb[e_i % 2]
                gsem = "GA" if e_i % 2 == 0 else "GB"
                vdma = b.op("gpsimd", lambda e, e_i=e_i, buf=buf: e.dma_start(
                    out=buf[:, :],
                    in_=wsh_pk[e_i * 128:(e_i + 1) * 128, :]),
                    deps=[("P", gen_done[e_i % 2])], inc=gsem, inc_by=16)
                for ti in range(NTT):
                    bank = (e_i * NTT + ti) % 2
                    src_t = ((ti + NTT // 2) % NTT) * TT
                    mmdeps = [(gsem, vdma), bank_free[bank]]
                    vmm = None
                    for k in range(NK):
                        vmm = b.op("tensor", lambda e, k=k, e_i=e_i, buf=buf, bank=bank,
                                   src_t=src_t, first=(k == 0), last=(k == NK - 1):
                                   e.matmul(ps[bank][:, 0:TT],
                                            buf[:, k * 128:(k + 1) * 128],
                                            hb[:, k * T + src_t:k * T + src_t + TT],
                                            start=first, stop=last),
                                   deps=mmdeps, inc="P")
                        mmdeps = []
                    bank_ready[bank] = vmm
                    gen_done[e_i % 2] = vmm
                    scr = f_scr[2 + (ti % 2)]
                    vh = b.op("vector", lambda e, e_i=e_i, ti=ti, scr=scr: e.tensor_scalar(
                        out=scr[:, :], in0=hb[:, e_i * T + ti * TT:e_i * T + ti * TT + TT],
                        scalar1=one_m_g[:, e_i:e_i + 1], scalar2=None, op0=ALU.mult),
                        inc="V")
                    vev = b.op("vector", lambda e, e_i=e_i, ti=ti, bank=bank, scr=scr:
                               e.scalar_tensor_tensor(
                                   out=x_shift[:, e_i * T + ti * TT:e_i * T + ti * TT + TT],
                                   in0=ps[bank][:, 0:TT], scalar=g_sig[:, e_i:e_i + 1],
                                   in1=scr[:, :], op0=ALU.mult, op1=ALU.add),
                               deps=[("P", vmm), ("V", vh)], inc="V")
                    bank_free[bank] = ("V", vev)
            b.run_phase("p2_shift")
        st_hb.close()

        with contextlib.ExitStack() as ph:
            kv_all = sb(ph, "kv_all", [S, 2 * T], F32)
            wkv = sb(ph, "wkv", [S, T], F32)
            wk_sb = sb(ph, "wk_sb", [128, NK * S], BF16)
            wv_sb = sb(ph, "wv_sb", [128, NK * S], BF16)
            xn = sb(ph, "xn", [128, NK * TT], BF16)
            b.op("gpsimd", lambda e: e.dma_start(out=wk_sb[:, :], in_=wk_pk[:, :]),
                 inc="GC", inc_by=16)
            b.op("gpsimd", lambda e: e.dma_start(out=wv_sb[:, :], in_=wv_pk[:, :]),
                 inc="GC", inc_by=16)
            v_wkv = b.cnt["GC"]
            for ti in range(NTT):
                getx = lambda k, ti=ti: x_shift[:, k * T + ti * TT:k * T + ti * TT + TT]
                bBM, bBR, vbm, vbr = ln_stats_and_bcast(getx, [], (0, 1, 2, 3))
                lastap = 0
                for k in range(NK):
                    lastap = ln_apply(
                        getx(k), xn[:, k * TT:(k + 1) * TT], bBM, bBR,
                        [("P", max(vbm, vbr))],
                        lng[:, k:k + 1], lnb[:, k:k + 1], f_scr[k % 2])
                bank_free[bBM] = bank_free[bBR] = ("V", lastap)
                for out_i, w_t in [(0, wk_sb), (1, wv_sb)]:
                    bank = 4 + out_i
                    mmdeps = [("GC", v_wkv), ("V", lastap), bank_free[bank]]
                    vmm = None
                    for k in range(NK):
                        vmm = b.op("tensor", lambda e, k=k, w_t=w_t, bank=bank,
                                   first=(k == 0), last=(k == NK - 1):
                                   e.matmul(ps[bank][0:S, 0:TT],
                                            w_t[:, k * S:(k + 1) * S],
                                            xn[:, k * TT:(k + 1) * TT],
                                            start=first, stop=last),
                                   deps=mmdeps, inc="P")
                        mmdeps = []
                    bank_ready[bank] = vmm
                    vev = b.op("vector", lambda e, ti=ti, out_i=out_i, bank=bank:
                               e.tensor_copy(
                                   out=kv_all[:, out_i * T + ti * TT:out_i * T + ti * TT + TT],
                                   in_=ps[bank][0:S, 0:TT]),
                               deps=[("P", vmm)], inc="V")
                    bank_free[bank] = ("V", vev)
            v_kv = b.cnt["V"]
            vsg = b.op("scalar", lambda e: e.activation(
                out=wkv[:, :], in_=kv_all[:, 0:T], func=ACT.Sigmoid),
                deps=[("V", v_kv)], inc="SC")
            vts = b.op("vector", lambda e: e.tensor_scalar(
                out=wkv[:, :], in0=wkv[:, :], scalar1=u_neg[:, :], scalar2=None,
                op0=ALU.mult), deps=[("SC", vsg)], inc="V")
            vex = b.op("scalar", lambda e: e.activation(
                out=wkv[:, :], in_=wkv[:, :], func=ACT.Exp),
                deps=[("V", vts)], inc="SC")
            v_wkvd = b.op("vector", lambda e: e.tensor_tensor(
                out=wkv[:, :], in0=wkv[:, :], in1=kv_all[:, T:2 * T], op=ALU.mult),
                deps=[("SC", vex)], inc="V")
            wrep = sb(ph, "wrep", [S, T], F32)
            scanb = sb(ph, "scanb", [S, TO], F32)
            carry = sb(ph, "carry", [S, 1], F32)
            states = sb(ph, "states", [S, TO], F32)
            vwr = b.op("vector", lambda e: e.tensor_scalar(
                out=wrep[:, :], in0=wkv[:, :], scalar1=0.0, scalar2=w_dec[:, :],
                op0=ALU.mult, op1=ALU.add), deps=[("V", v_wkvd)], inc="V")
            vs1 = b.op("vector", lambda e: e.tensor_tensor_scan(
                out=scanb[:, :], data0=wrep[:, TO:T], data1=wkv[:, TO:T],
                initial=0.0, op0=ALU.mult, op1=ALU.add),
                deps=[("V", vwr)], inc="V")
            vc1 = b.op("vector", lambda e: e.tensor_tensor(
                out=carry[:, :], in0=scanb[:, TO - 1:TO], in1=cmask[:, :],
                op=ALU.mult), deps=[("V", vs1)], inc="V")
            vs2 = b.op("vector", lambda e: e.tensor_tensor_scan(
                out=states[:, :], data0=wrep[:, 0:TO], data1=wkv[:, 0:TO],
                initial=carry[:, :], op0=ALU.mult, op1=ALU.add),
                deps=[("V", vc1)], inc="V")
            v_stb = b.op("vector", lambda e: e.tensor_copy(
                out=states_bf[:, :], in_=states[:, :]), deps=[("V", vs2)], inc="V")
            if c.debug:
                b.op("sync", lambda e: e.dma_start(out=dbg["dbg_states"][:, :],
                     in_=states[:, :]), deps=[("V", vs2)], inc="HO", inc_by=16)
                b.need("sync", [("HO", b.cnt["HO"])])
            b.run_phase("p3a_kv")
        st_xs.close()

        xmid = sb(st_xm, "xmid_tm", [128, NTC * D], F32)
        with contextlib.ExitStack() as ph:
            wo_sb = sb(ph, "wo_sb", [S, D], BF16)
            xo = [sb(ph, f"xo{i}", [128, D], F32) for i in range(2)]
            b.op("gpsimd", lambda e: e.dma_start(out=wo_sb[:, :], in_=woutT[:, :]),
                 inc="GC", inc_by=16)
            v_wo = b.cnt["GC"]
            xo_done = {0: 0, 1: 0}
            scr_free = {0: ("Q", 0), 1: ("Q", 0), 2: ("Q", 0), 3: ("Q", 0)}
            for tc in range(NTC):
                buf = xo[tc % 2]
                hxs = "HX" if tc % 2 == 0 else "HY"
                vdma = b.op("sync", lambda e, tc=tc, buf=buf: e.dma_start(
                    out=buf[:, :], in_=x_tm_h[tc * 128:(tc + 1) * 128, :]),
                    deps=[("G", xo_done[tc % 2])], inc=hxs, inc_by=16)
                for dt in range(4):
                    bank = (tc * 4 + dt) % 4
                    vmm = b.op("tensor", lambda e, tc=tc, dt=dt, bank=bank: e.matmul(
                        ps[bank][:, 0:TT], states_bf[0:S, tc * 128:(tc + 1) * 128],
                        wo_sb[0:S, dt * TT:(dt + 1) * TT], start=True, stop=True),
                        deps=[("GC", v_wo), bank_free[bank]], inc="P")
                    bank_ready[bank] = vmm
                    scr = f_scr[dt % 4]
                    vmu = b.op("vector", lambda e, dt=dt, bank=bank, scr=scr:
                               e.tensor_tensor(
                                   out=scr[:, :], in0=ps[bank][:, 0:TT],
                                   in1=ga_tm[:, dt * TT:(dt + 1) * TT], op=ALU.mult),
                               deps=[("P", vmm), scr_free[dt]], inc="V")
                    vad = b.op("gpsimd", lambda e, tc=tc, dt=dt, scr=scr, buf=buf:
                               e.tensor_tensor(
                                   out=xmid[:, tc * D + dt * TT:tc * D + dt * TT + TT],
                                   in0=scr[:, :], in1=buf[:, dt * TT:(dt + 1) * TT],
                                   op=ALU.add),
                               deps=[("V", vmu), (hxs, vdma)], inc="G")
                    bank_free[bank] = ("G", vad)
                    scr_free[dt] = ("G", vad)
                    xo_done[tc % 2] = vad
            if c.debug:
                gg = b.cnt["G"]
                b.op("sync", lambda e: e.dma_start(out=dbg["dbg_xmid"][:, :],
                     in_=xmid[:, :]), deps=[("G", gg)], inc="HO", inc_by=16)
                b.need("sync", [("HO", b.cnt["HO"])])
            b.run_phase("p3b_res")
        st_mix.close()

        x2f = sb(st_xm, "x2f", [128, NK * NTC * 128], BF16)
        qguard2 = qguard
        c1275 = sb(st_xm, "c1275", [128, 1], F32)
        bstep = sb(st_xm, "bstep", [128, 1], F32)
        REPall = sb(st_xm, "REPall", [64, 16 * 128], BF16)
        QF = 512
        qsc = []
        for i in range(2):
            qsc.append(dict(
                A=sb(st_xm, f"qA{i}", [128, QF], F32),
                Bq=sb(st_xm, f"qB{i}", [128, QF], F32),
                Cq=sb(st_xm, f"qC{i}", [128, QF], F32),
                X=sb(st_xm, f"qX{i}", [128, QF], BF16),
                X2=sb(st_xm, f"qX2{i}", [128, QF], BF16),
                X3=sb(st_xm, f"qX3{i}", [128, QF], BF16),
                Xi=sb(st_xm, f"qXi{i}", [128, QF], mybir.dt.int16),
                Xc=sb(st_xm, f"qXc{i}", [128, QF], mybir.dt.int16),
                sm=sb(st_xm, f"qsm{i}", [128, 48], F32),
                busy=[{"V": 0, "SC": 0, "G": 0}],
            ))
        qset_rr = [0]
        b.op("gpsimd", lambda e: e.memset(c1275[:, :], 127.5), inc="Q")
        b.op("gpsimd", lambda e: e.memset(bstep[:, :], -131.0 * LN2), inc="Q")
        b.op("gpsimd", lambda e: e.iota(
            REPall[:, :].rearrange("k (hc a bb) -> k hc a bb", hc=16, a=4),
            [[4, 16], [1, 4], [0, 32]], channel_multiplier=-1,
            allow_small_or_imprecise_dtypes=True), inc="Q")
        v_q5 = b.cnt["Q"]
        v_rep = b.op("vector", lambda e: e.tensor_scalar(
            out=REPall[:, :], in0=REPall[:, :], scalar1=0.0, scalar2=None,
            op0=ALU.is_equal), deps=[("Q", v_q5)], inc="V")

        def chain_v3(src_ap, dst_ap, F, src_deps, inv_src, s_src, amax_into_sm):
            si = qset_rr[0]
            qset_rr[0] = 1 - si
            q = qsc[si]
            NB = F // 32
            A, Bq, Cq, X, X2, X3, Xi, Xc, sm = (
                q["A"], q["Bq"], q["Cq"], q["X"], q["X2"], q["X3"],
                q["Xi"], q["Xc"], q["sm"])
            bz = q["busy"][0]
            gate = [("V", bz["V"]), ("SC", bz["SC"]), ("G", bz["G"])]
            s3 = lambda t: t[:, 0:F].rearrange("p (b j) -> p b j", j=32)
            deps0 = list(src_deps) + gate
            if amax_into_sm:
                b.op("vector", lambda e: e.tensor_reduce(
                    out=sm[:, 0:NB], in_=s3(src_ap), axis=AXX, op=ALU.max,
                    apply_absolute_value=True), deps=deps0, inc="V")
                b.op("vector", lambda e: e.tensor_scalar(
                    out=sm[:, 16:16 + NB], in0=sm[:, 0:NB], scalar1=1.0 / 448.0,
                    scalar2=1e-30, op0=ALU.mult, op1=ALU.max), inc="V")
                v_inv = b.op("vector", lambda e: e.reciprocal(
                    out=sm[:, 32:32 + NB], in_=sm[:, 16:16 + NB]), inc="V")
                deps_y = deps0 + [("V", v_inv)]
            else:
                deps_y = deps0
            y_eng = "gpsimd" if amax_into_sm else "vector"
            y_sem = "G" if amax_into_sm else "V"
            g_y = b.op(y_eng, lambda e: e.tensor_tensor(
                out=(s3(A) if amax_into_sm else A[:, 0:F]),
                in0=(s3(src_ap) if amax_into_sm else src_ap),
                in1=inv_src(sm), op=ALU.mult), deps=deps_y, inc=y_sem)
            s_ya = b.op("scalar", lambda e: e.activation(
                out=X[:, 0:F], in_=A[:, 0:F], func=ACT.Abs,
                bias=qguard2[:, 0:1]), deps=[(y_sem, g_y)] + gate, inc="SC")
            s_l = b.op("scalar", lambda e: e.activation(
                out=Bq[:, 0:F], in_=X[:, 0:F], func=ACT.Ln), inc="SC")
            s_i1 = b.op("scalar", lambda e: e.activation(
                out=Xi[:, 0:F], in_=Bq[:, 0:F], func=ACT.Copy,
                scale=1.0 / LN2, bias=128.0), inc="SC")
            g_d1 = b.op("vector", lambda e: e.scalar_tensor_tensor(
                out=Cq[:, 0:F], in0=Bq[:, 0:F], scalar=1.0 / LN2,
                in1=Xi[:, 0:F], op0=ALU.mult, op1=ALU.subtract),
                deps=[("SC", s_i1)], inc="V")
            v_c1 = b.op("gpsimd", lambda e: e.tensor_scalar(
                out=Xc[:, 0:F], in0=Cq[:, 0:F], scalar1=-128.0, scalar2=None,
                op0=ALU.is_lt), deps=[("V", g_d1)], inc="G")
            g_E = b.op("vector", lambda e: e.tensor_tensor(
                out=Xi[:, 0:F], in0=Xi[:, 0:F], in1=Xc[:, 0:F],
                op=ALU.subtract), deps=[("G", v_c1)], inc="V")
            s_st = b.op("scalar", lambda e: e.activation(
                out=X2[:, 0:F], in_=Xi[:, 0:F], func=ACT.Exp, scale=LN2,
                bias=bstep[:, 0:1]), deps=[("V", g_E)], inc="SC")
            def _recip(e):
                with nc.allow_low_precision(reason="pow2 reciprocal exact in bf16"):
                    return e.reciprocal(out=X3[:, 0:F], in_=X2[:, 0:F])
            v_is = b.op("vector", _recip, deps=[("SC", s_st)], inc="V")
            v_yq = b.op("gpsimd", lambda e: e.tensor_tensor(
                out=Cq[:, 0:F], in0=A[:, 0:F], in1=X3[:, 0:F], op=ALU.mult),
                deps=[("V", max(v_is, g_E))], inc="G")
            s_i2 = b.op("scalar", lambda e: e.activation(
                out=Xc[:, 0:F], in_=Cq[:, 0:F], func=ACT.Copy,
                bias=24.5), deps=[("G", v_yq)], inc="SC")
            g_d2 = b.op("vector", lambda e: e.scalar_tensor_tensor(
                out=A[:, 0:F], in0=Cq[:, 0:F], scalar=24.5,
                in1=Xc[:, 0:F], op0=ALU.add, op1=ALU.subtract),
                deps=[("SC", s_i2)], inc="V")
            v_c2 = b.op("gpsimd", lambda e: e.tensor_scalar(
                out=Xi[:, 0:F], in0=A[:, 0:F], scalar1=0.0, scalar2=None,
                op0=ALU.is_lt), deps=[("V", g_d2)], inc="G")
            g_r = b.op("vector", lambda e: e.tensor_tensor(
                out=Xc[:, 0:F], in0=Xc[:, 0:F], in1=Xi[:, 0:F],
                op=ALU.subtract), deps=[("G", v_c2)], inc="V")
            g_q = b.op("vector", lambda e: e.scalar_tensor_tensor(
                out=Bq[:, 0:F], in0=Xc[:, 0:F], scalar=-24.0,
                in1=X2[:, 0:F], op0=ALU.add, op1=ALU.mult),
                deps=[("V", g_r)], inc="V")
            v_dq = b.op("vector", lambda e: e.tensor_tensor(
                out=(s3(dst_ap) if amax_into_sm else dst_ap),
                in0=(s3(Bq) if amax_into_sm else Bq[:, 0:F]),
                in1=s_src(sm), op=ALU.mult), deps=[("V", g_q)], inc="V")
            q["busy"][0] = {"V": v_dq, "SC": s_i2, "G": v_c2}
            return v_dq, [("V", b.cnt["V"]), (y_sem, g_y)]

        def emit_quant(src_ap, dst_ap, F, src_deps):
            NB = F // 32
            inv_src = lambda sm: (sm[:, 32:32 + NB].rearrange("p b -> p b ()")
                                  .broadcast_to((128, NB, 32)))
            s_src = lambda sm: (sm[:, 16:16 + NB].rearrange("p b -> p b ()")
                                .broadcast_to((128, NB, 32)))
            return chain_v3(src_ap, dst_ap, F, src_deps, inv_src, s_src, True)

        def emit_quant_ps(src_ap, dst_ap, F, src_deps, inv_ps, s_ps):
            return chain_v3(src_ap, dst_ap, F, src_deps,
                            lambda sm: inv_ps, lambda sm: s_ps, False)

        with contextlib.ExitStack() as ph:
            h2t = [sb(ph, f"h2t{i}", [128, D], F32) for i in range(2)]
            dqx = [sb(ph, f"dqx{i}", [128, D], BF16) for i in range(2)]
            sqf = sb(ph, "sqf", [128, D], F32)
            stt_s = sb(ph, "stt_s", [128, 8], F32)
            h2_free = {0: [("Q", 0)], 1: [("Q", 0)]}
            dq_free = {0: ("Q", 0), 1: ("Q", 0)}
            for tc in range(NTC):
                xsl = lambda tc=tc: xmid[:, tc * D:(tc + 1) * D]
                v_s1 = b.op("vector", lambda e, xsl=xsl: e.tensor_reduce(
                    out=stt_s[:, 0:1], in_=xsl(), axis=AXX, op=ALU.add), inc="V")
                s_sq = b.op("scalar", lambda e, xsl=xsl: e.activation(
                    out=sqf[:, :], in_=xsl(), func=ACT.Square),
                    deps=[("V", b.cnt["V"])], inc="SC")
                v_s2 = b.op("vector", lambda e: e.tensor_reduce(
                    out=stt_s[:, 1:2], in_=sqf[:, :], axis=AXX, op=ALU.add),
                    deps=[("SC", s_sq)], inc="V")
                b.op("vector", lambda e: e.tensor_scalar(
                    out=stt_s[:, 2:3], in0=stt_s[:, 0:1], scalar1=1.0 / D,
                    scalar2=None, op0=ALU.mult), inc="V")
                b.op("vector", lambda e: e.tensor_scalar(
                    out=stt_s[:, 3:4], in0=stt_s[:, 1:2], scalar1=1.0 / D,
                    scalar2=None, op0=ALU.mult), inc="V")
                b.op("vector", lambda e: e.tensor_tensor(
                    out=stt_s[:, 4:5], in0=stt_s[:, 2:3], in1=stt_s[:, 2:3],
                    op=ALU.mult), inc="V")
                b.op("vector", lambda e: e.scalar_tensor_tensor(
                    out=stt_s[:, 5:6], in0=stt_s[:, 3:4], scalar=LN_EPS,
                    in1=stt_s[:, 4:5], op0=ALU.add, op1=ALU.subtract), inc="V")
                s_sd = b.op("scalar", lambda e: e.activation(
                    out=stt_s[:, 6:7], in_=stt_s[:, 5:6], func=ACT.Sqrt),
                    deps=[("V", b.cnt["V"])], inc="SC")
                v_ri = b.op("vector", lambda e: e.reciprocal(
                    out=stt_s[:, 7:8], in_=stt_s[:, 6:7]), deps=[("SC", s_sd)], inc="V")
                hbuf = h2t[tc % 2]
                v_z = b.op("vector", lambda e, xsl=xsl, hbuf=hbuf: e.tensor_scalar(
                    out=hbuf[:, :], in0=xsl(), scalar1=stt_s[:, 2:3],
                    scalar2=stt_s[:, 7:8], op0=ALU.subtract, op1=ALU.mult),
                    deps=[("V", v_ri)] + list(h2_free[tc % 2]), inc="V")
                v_h1 = b.op("gpsimd", lambda e, hbuf=hbuf: e.tensor_tensor(
                    out=hbuf[:, :], in0=hbuf[:, :], in1=cb1_tm[:, :], op=ALU.mult),
                    deps=[("V", v_z)], inc="G")
                v_h2 = b.op("vector", lambda e, hbuf=hbuf: e.tensor_tensor(
                    out=hbuf[:, :], in0=hbuf[:, :], in1=sb_tm[:, :], op=ALU.add),
                    deps=[("G", v_h1)], inc="V")
                if c.debug and tc == 0:
                    b.op("sync", lambda e, hbuf=hbuf: e.dma_start(
                        out=dbg["dbg_h2t0"][:, :], in_=hbuf[:, :]),
                        deps=[("V", v_h2)], inc="HO", inc_by=16)
                dqb = dqx[tc % 2]
                halves = []
                for hf in range(4):
                    dv, _ = emit_quant(hbuf[:, hf * 512:(hf + 1) * 512],
                                       dqb[:, hf * 512:(hf + 1) * 512], 512,
                                       [("V", v_h2), dq_free[tc % 2]])
                    halves.append(dv)
                h2_free[tc % 2] = [("V", max(halves))]
                if c.debug and tc == 0:
                    h2_free[0].append(("HO", b.cnt["HO"]))
                if c.debug and tc == 0:
                    b.need("sync", [("HO", b.cnt["HO"])])
                    b.op("sync", lambda e, dqb=dqb: e.dma_start(
                        out=dbg["dbg_dqx0"][:, :], in_=dqb[:, :]),
                        deps=[("V", max(halves))], inc="HO", inc_by=16)
                txs = "TX0" if tc % 2 == 0 else "TX1"
                vt = None
                exdep = ([("HO", b.cnt["HO"])] if (c.debug and tc == 0) else [])
                for hc in range(NK):
                    vt = b.op("sync", lambda e, tc=tc, hc=hc, dqb=dqb:
                              e.dma_start_transpose(
                                  out=x2f[:, (hc * NTC + tc) * 128:
                                          (hc * NTC + tc) * 128 + 128],
                                  in_=dqb[:, hc * 128:(hc + 1) * 128]),
                              deps=[("V", max(halves))] + exdep,
                              inc=txs, inc_by=16)
                    exdep = []
                dq_free[tc % 2] = (txs, vt)
            b.need("sync", [("TX0", b.cnt["TX0"]), ("TX1", b.cnt["TX1"])])
            if c.debug:
                b.op("sync", lambda e: e.dma_start(out=dbg["dbg_x2f"][:, :],
                     in_=x2f[:, :]), inc="HO", inc_by=16)
                b.need("sync", [("HO", b.cnt["HO"])])
            b.run_phase("p4_ln3")
        st_p4.close()
        x2f_deps = [("TX0", b.cnt["TX0"]), ("TX1", b.cnt["TX1"])]

        with contextlib.ExitStack() as ph:
            Rrm = [sb(ph, f"Rrm{i}", [128, 1024], F32) for i in range(2)]
            Ffm = Rrm
            amx = sb(ph, "amx", [128, 64], F32)
            sPd = [sb(ph, f"sPd{i}", [128, 128], BF16) for i in range(2)]
            sT_w = sb(ph, "sT_w", [128, 2 * TT], BF16)
            iT_w = sb(ph, "iT_w", [128, 2 * TT], BF16)
            sT_d = sb(ph, "sT_d", [128, D], BF16)
            iT_d = sb(ph, "iT_d", [128, D], BF16)
            gwf = sb(ph, "gwf", [128, NK * TT], BF16)
            upf = sb(ph, "upf", [128, NK * TT], BF16)
            dwf = sb(ph, "dwf", [128, NK * TT], BF16)
            hfq = [sb(ph, "hfq0", [128, NTC * 4 * 128], BF16)]
            htl = [sb(ph, "htl0", [128, TT], BF16)]
            hsl = [sb(ph, "hsl0", [128, TT], BF16)]
            dqh = [sb(ph, "dqh0", [128, TT], BF16)]


            b.op("gpsimd", lambda e: e.memset(sPd[0][:, :], 1.0), inc="Q")
            b.op("gpsimd", lambda e: e.memset(sPd[1][:, :], 1.0), inc="Q")
            v_qpd = b.cnt["Q"]
            Rrm_free = {0: [("Q", 0)], 1: [("Q", 0)]}
            Ffm_free = Rrm_free
            sPd_free = {0: ("Q", 0), 1: ("Q", 0)}
            dqh_free = {0: [("Q", 0)], 1: [("Q", 0)]}
            htl_free = {0: ("Q", 0), 1: ("Q", 0)}
            hsl_free = {0: ("Q", 0), 1: ("Q", 0)}
            gwf_free_P = 0
            upf_free_P = 0

            dwf_free_P = 0
            hfq_free_P = {0: 0, 1: 0}
            sT_free = {"w": ("Q", 0), "d": ("Q", 0)}
            hid = [0]
            rid = [0]
            fid = rid
            NQ = c.NQ
            inv_bank = [0]
            s_bank = [0]

            for qd in range(NQ):
                for wsel, wsrc in [(0, gw_r), (1, up_r)]:
                    for cc in range(4):
                        ic = qd * 4 + cc
                        v_am = 0
                        for half in range(2):
                            ri = rid[0]
                            rid[0] = 1 - ri
                            vld = b.op("sync", lambda e, wsrc=wsrc, ic=ic, ri=ri,
                                       half=half: e.dma_start(
                                           out=Rrm[ri][:, :],
                                           in_=wsrc[ic * 128:(ic + 1) * 128,
                                                    half * 1024:(half + 1) * 1024]),
                                       deps=Rrm_free[ri],
                                       inc="RL0" if ri == 0 else "RL1", inc_by=16)
                            v_am = b.op("vector", lambda e, ri=ri, half=half:
                                        e.tensor_reduce(
                                            out=amx[:, half * 32:half * 32 + 32],
                                            in_=Rrm[ri][:, :].rearrange(
                                                "p (b j) -> p b j", j=32),
                                            axis=AXX, op=ALU.max,
                                            apply_absolute_value=True),
                                        deps=[("RL0" if ri == 0 else "RL1", vld)],
                                        inc="V")
                            Rrm_free[ri] = [("V", v_am)]
                        si = cc % 2
                        v_s = b.op("vector", lambda e, si=si: e.tensor_scalar(
                            out=sPd[si][:, 0:64], in0=amx[:, 0:64],
                            scalar1=1.0 / 448.0, scalar2=1e-30,
                            op0=ALU.mult, op1=ALU.max),
                            deps=[sPd_free[si], ("Q", v_qpd)], inc="V")
                        vtr = b.op("scalar", lambda e, wsel=wsel, cc=cc, si=si:
                                   e.dma_start_transpose(
                                       out=sT_w[:, wsel * TT + cc * 128:
                                               wsel * TT + cc * 128 + 128],
                                       in_=sPd[si][:, :]),
                                   deps=[("V", v_s), sT_free["w"]],
                                   inc="TS0" if si == 0 else "TS1", inc_by=16)
                        sPd_free[si] = ("TS0" if si == 0 else "TS1", vtr)
                for hc in range(NK):
                    ri = rid[0]
                    rid[0] = 1 - ri
                    vld = b.op("sync", lambda e, hc=hc, qd=qd, ri=ri: e.dma_start(
                        out=Rrm[ri][:, 0:TT],
                        in_=dw_r[hc * 128:(hc + 1) * 128, qd * TT:(qd + 1) * TT]),
                        deps=Rrm_free[ri],
                        inc="RL0" if ri == 0 else "RL1", inc_by=16)
                    v_am = b.op("vector", lambda e, ri=ri: e.tensor_reduce(
                        out=amx[:, 0:16],
                        in_=Rrm[ri][:, 0:TT].rearrange("p (b j) -> p b j", j=32),
                        axis=AXX, op=ALU.max, apply_absolute_value=True),
                        deps=[("RL0" if ri == 0 else "RL1", vld)], inc="V")
                    Rrm_free[ri] = [("V", v_am)]
                    si = hc % 2
                    v_s = b.op("vector", lambda e, si=si: e.tensor_scalar(
                        out=sPd[si][:, 0:16], in0=amx[:, 0:16],
                        scalar1=1.0 / 448.0, scalar2=1e-30,
                        op0=ALU.mult, op1=ALU.max),
                        deps=[sPd_free[si], ("Q", v_qpd)], inc="V")
                    vtr = b.op("scalar", lambda e, hc=hc, si=si:
                               e.dma_start_transpose(
                                   out=sT_d[:, hc * 128:(hc + 1) * 128],
                                   in_=sPd[si][:, :]),
                               deps=[("V", v_s), sT_free["d"]],
                               inc="TS0" if si == 0 else "TS1", inc_by=16)
                    sPd_free[si] = ("TS0" if si == 0 else "TS1", vtr)
                pass
                def _ri(e, dst, src):
                    with nc.allow_low_precision(reason="bf16 scale reciprocal"):
                        return e.reciprocal(out=dst, in_=src)
                ts_tot = [("TS0", b.cnt["TS0"]), ("TS1", b.cnt["TS1"])]
                v_iw = b.op("vector", lambda e: _ri(e, iT_w[:, :], sT_w[:, :]),
                            deps=ts_tot + [sT_free["w"]], inc="V")
                v_id2 = b.op("vector", lambda e: _ri(e, iT_d[:, :], sT_d[:, :]),
                             deps=ts_tot + [sT_free["d"]], inc="V")
                v_id2 = b.op("gpsimd", lambda e: e.tensor_tensor(
                    out=sT_d[:, :], in0=sT_d[:, :], in1=gb_tm[:, :],
                    op=ALU.mult), deps=[("V", v_id2)], inc="G")

                last_dq_w = 0
                for wsel, (wsrc, fdst, fp) in enumerate(
                        [(gwT_h, gwf, gwf_free_P), (upT_h, upf, upf_free_P)]):
                    for hc in range(NK):
                        fi = fid[0]
                        fid[0] = 1 - fi
                        vld = b.op("scalar", lambda e, wsrc=wsrc, hc=hc, qd=qd, fi=fi:
                                   e.dma_start(
                                       out=Ffm[fi][:, 0:TT],
                                       in_=wsrc[hc * 128:(hc + 1) * 128,
                                                qd * TT:(qd + 1) * TT]),
                                   deps=Ffm_free[fi],
                                   inc="FL0" if fi == 0 else "FL1", inc_by=16)
                        ib = 4 + inv_bank[0]
                        inv_bank[0] = 1 - inv_bank[0]
                        vmi = b.op("tensor", lambda e, ib=ib, hc=hc, wsel=wsel:
                                   e.matmul(ps[ib][:, 0:TT],
                                            REPall[:, hc * 128:(hc + 1) * 128],
                                            iT_w[0:64,
                                                 wsel * TT:(wsel + 1) * TT],
                                            start=True, stop=True),
                                   deps=[("V", max(v_iw, v_rep)), bank_free[ib]],
                                   inc="P")
                        bank_ready[ib] = vmi
                        sb_ = 6 + s_bank[0]
                        s_bank[0] = 1 - s_bank[0]
                        vms = b.op("tensor", lambda e, sb_=sb_, hc=hc, wsel=wsel:
                                   e.matmul(ps[sb_][:, 0:TT],
                                            REPall[:, hc * 128:(hc + 1) * 128],
                                            sT_w[0:64,
                                                 wsel * TT:(wsel + 1) * TT],
                                            start=True, stop=True),
                                   deps=[bank_free[sb_]], inc="P")
                        bank_ready[sb_] = vms
                        dv, fr = emit_quant_ps(
                            Ffm[fi][:, 0:TT],
                            fdst[:, hc * TT:(hc + 1) * TT], TT,
                            [("FL0" if fi == 0 else "FL1", vld),
                             ("P", max(vmi, vms)),
                             ("P", fp if hc == 0 else 0)],
                            ps[ib][:, 0:TT], ps[sb_][:, 0:TT])
                        Ffm_free[fi] = [fr[1]]
                        bank_free[ib] = fr[1]
                        bank_free[sb_] = ("V", dv)
                        last_dq_w = max(last_dq_w, dv)
                v_wq = last_dq_w
                last_dq_d = 0
                for icr in range(4):
                    for ht in range(4):
                        fi = fid[0]
                        fid[0] = 1 - fi
                        ic = qd * 4 + icr
                        vld = b.op("scalar", lambda e, ic=ic, ht=ht, fi=fi:
                                   e.dma_start(
                                       out=Ffm[fi][:, 0:TT],
                                       in_=dwT_h[ic * 128:(ic + 1) * 128,
                                                 ht * TT:(ht + 1) * TT]),
                                   deps=Ffm_free[fi],
                                   inc="FL0" if fi == 0 else "FL1", inc_by=16)
                        ib = 4 + inv_bank[0]
                        inv_bank[0] = 1 - inv_bank[0]
                        vmi = b.op("tensor", lambda e, ib=ib, icr=icr, ht=ht:
                                   e.matmul(ps[ib][:, 0:TT],
                                            REPall[:, icr * 128:(icr + 1) * 128],
                                            iT_d[0:64,
                                                 ht * TT:(ht + 1) * TT],
                                            start=True, stop=True),
                                   deps=[("V", max(v_id2, v_rep)), bank_free[ib]],
                                   inc="P")
                        bank_ready[ib] = vmi
                        sb_ = 6 + s_bank[0]
                        s_bank[0] = 1 - s_bank[0]
                        vms = b.op("tensor", lambda e, sb_=sb_, icr=icr, ht=ht:
                                   e.matmul(ps[sb_][:, 0:TT],
                                            REPall[:, icr * 128:(icr + 1) * 128],
                                            sT_d[0:64,
                                                 ht * TT:(ht + 1) * TT],
                                            start=True, stop=True),
                                   deps=[bank_free[sb_], ("G", v_id2)], inc="P")
                        bank_ready[sb_] = vms
                        dv, fr = emit_quant_ps(
                            Ffm[fi][:, 0:TT],
                            dwf[:, (icr * 4 + ht) * TT:(icr * 4 + ht) * TT + TT], TT,
                            [("FL0" if fi == 0 else "FL1", vld),
                             ("P", max(vmi, vms)),
                             ("P", dwf_free_P if (icr == 0 and ht == 0) else 0)],
                            ps[ib][:, 0:TT], ps[sb_][:, 0:TT])
                        Ffm_free[fi] = [fr[1]]
                        bank_free[ib] = fr[1]
                        bank_free[sb_] = ("V", dv)
                        last_dq_d = max(last_dq_d, dv)
                v_dq_d = last_dq_d
                sT_free["w"] = ("P", b.cnt["P"])
                sT_free["d"] = ("P", b.cnt["P"])

                hbuf = hfq[0]
                for tc in range(NTC):
                    bg, bu = 0, 1
                    mmdeps = [("V", v_wq)] + x2f_deps + [bank_free[bg]]
                    vg = None
                    for hc in range(NK):
                        vg = b.op("tensor", lambda e, tc=tc, hc=hc, bg=bg,
                                  first=(hc == 0), last=(hc == NK - 1):
                                  e.matmul(ps[bg][:, 0:TT],
                                           x2f[:, (hc * NTC + tc) * 128:
                                               (hc * NTC + tc) * 128 + 128],
                                           gwf[:, hc * TT:(hc + 1) * TT],
                                           start=first, stop=last),
                                  deps=mmdeps, inc="P")
                        mmdeps = []
                    mmdeps = [("V", v_wq), bank_free[bu]]
                    vu = None
                    for hc in range(NK):
                        vu = b.op("tensor", lambda e, tc=tc, hc=hc, bu=bu,
                                  first=(hc == 0), last=(hc == NK - 1):
                                  e.matmul(ps[bu][:, 0:TT],
                                           x2f[:, (hc * NTC + tc) * 128:
                                               (hc * NTC + tc) * 128 + 128],
                                           upf[:, hc * TT:(hc + 1) * TT],
                                           start=first, stop=last),
                                  deps=mmdeps, inc="P")
                        mmdeps = []
                    bank_ready[bg], bank_ready[bu] = vg, vu
                    if tc == NTC - 1:
                        gwf_free_P = vg
                        upf_free_P = vu
                    hi = 0
                    s_si = b.op("scalar", lambda e, bg=bg, hi=hi: e.activation(
                        out=hsl[hi][:, :], in_=ps[bg][:, 0:TT], func=ACT.Silu),
                        deps=[("P", vg), hsl_free[hi]], inc="SC")
                    bank_free[bg] = ("SC", s_si)
                    g_h = b.op("vector", lambda e, bu=bu, hi=hi: e.tensor_tensor(
                        out=htl[hi][:, :], in0=hsl[hi][:, :], in1=ps[bu][:, 0:TT],
                        op=ALU.mult), deps=[("SC", s_si), ("P", vu), htl_free[hi]],
                        inc="V")
                    bank_free[bu] = ("V", g_h)
                    hsl_free[hi] = ("V", g_h)
                    if c.debug and qd == 0 and tc == 0:
                        b.op("sync", lambda e, hi=hi: e.dma_start(
                            out=dbg["dbg_h0"][:, :], in_=htl[hi][:, :]),
                            deps=[("V", g_h)], inc="HO", inc_by=16)
                    dv, fr = emit_quant(htl[hi][:, :], dqh[hi][:, :], TT,
                                        [("V", g_h)] + list(dqh_free[hi]))
                    htl_free[hi] = fr[0] if not c.debug else ("HO", b.cnt["HO"])
                    if c.debug and qd == 0 and tc == 0:
                        b.need("sync", [("HO", b.cnt["HO"])])
                        b.op("sync", lambda e, hi=hi: e.dma_start(
                            out=dbg["dbg_hq0"][:, :], in_=dqh[hi][:, :]),
                            deps=[("V", dv)], inc="HO", inc_by=16)
                    ths = "TH0" if hi == 0 else "TH1"
                    vt = None
                    exdep = ([("HO", b.cnt["HO"])]
                             if (c.debug and qd == 0 and tc == 0) else [])
                    for ccx in range(4):
                        vt = b.op("sync", lambda e, tc=tc, hi=hi, hbuf=hbuf, ccx=ccx:
                                  e.dma_start_transpose(
                                      out=hbuf[:, (tc * 4 + ccx) * 128:
                                              (tc * 4 + ccx) * 128 + 128],
                                      in_=dqh[hi][:, ccx * 128:(ccx + 1) * 128]),
                                  deps=[("V", dv), ("P", hfq_free_P[0])] + exdep,
                                  inc=ths, inc_by=16)
                        exdep = []
                    dqh_free[hi] = [(ths, vt)]
                tr_hq = [("TH0", b.cnt["TH0"]), ("TH1", b.cnt["TH1"])]

                last_q = (qd == NQ - 1)
                vlast = None
                for tc in range(NTC):
                    for ht in range(4):
                        bank = 2 + ((tc * 4 + ht) % 2)
                        mmdeps = [("V", v_dq_d)] + tr_hq + [bank_free[bank]]
                        vd = None
                        for ccc in range(4):
                            vd = b.op("tensor", lambda e, tc=tc, ht=ht, cc=ccc,
                                      bank=bank, hbuf=hbuf,
                                      first=(ccc == 0), last=(ccc == 3):
                                      e.matmul(ps[bank][:, 0:TT],
                                               hbuf[:, (tc * 4 + cc) * 128:
                                                    (tc * 4 + cc) * 128 + 128],
                                               dwf[:, (cc * 4 + ht) * TT:
                                                   (cc * 4 + ht) * TT + TT],
                                               start=first, stop=last),
                                      deps=mmdeps, inc="P")
                            mmdeps = []
                        bank_ready[bank] = vd
                        vlast = vd
                        asl = lambda tc=tc, ht=ht: xmid[:, tc * D + ht * TT:
                                                        tc * D + ht * TT + TT]
                        g_ac = b.op("vector", lambda e, bank=bank, asl=asl:
                                    e.tensor_tensor(
                                        out=asl(), in0=asl(),
                                        in1=ps[bank][:, 0:TT], op=ALU.add),
                                    deps=[("P", vd)], inc="V")
                        bank_free[bank] = ("V", g_ac)
                        if last_q:
                            vho = b.op("sync", lambda e, tc=tc, ht=ht, asl=asl:
                                       e.dma_start(
                                           out=out_tm_h[tc * 128:(tc + 1) * 128,
                                                        ht * TT:(ht + 1) * TT],
                                           in_=asl()),
                                       deps=[("V", g_ac)], inc="HO0", inc_by=16)
                dwf_free_P = vlast
                hfq_free_P[0] = vlast
            b.need("sync", [("HO", b.cnt["HO"]), ("HO0", b.cnt["HO0"])])
            b.run_phase("p5_ffn")
        st_xm.close()
    return nc



def stage_inputs(c: Cfg, inputs):
    x = np.asarray(inputs["L0_attn_x"])
    D, T, S, NK, TO = c.D, c.T, c.S, c.NK, c.TO

    def pp(v):
        return np.ascontiguousarray(np.asarray(v).reshape(NK, 128).T)

    W_shift = np.asarray(inputs["L0_attn_W_shift"])
    wsh_pk = np.ascontiguousarray(
        W_shift.reshape(NK, 128, NK, 128).transpose(0, 3, 2, 1).reshape(NK * 128, NK * 128))
    W_key = np.asarray(inputs["L0_attn_W_key"])
    wk_pk = np.ascontiguousarray(
        W_key.reshape(S, NK, 128).transpose(2, 1, 0).reshape(128, NK * S))
    W_val = np.asarray(inputs["L0_attn_W_value"])
    wv_pk = np.ascontiguousarray(
        W_val.reshape(S, NK, 128).transpose(2, 1, 0).reshape(128, NK * S))

    shared = {
        "wsh_pk": wsh_pk,
        "wk_pk": wk_pk,
        "wv_pk": wv_pk,
        "woutT": np.ascontiguousarray(np.asarray(inputs["L0_attn_W_output"]).T),
        "gw_r": np.ascontiguousarray(np.asarray(inputs["L0_ffn_gate_w"])),
        "up_r": np.ascontiguousarray(np.asarray(inputs["L0_ffn_up_w"])),
        "dw_r": np.ascontiguousarray(np.asarray(inputs["L0_ffn_down_w"])),
        "gwT": np.ascontiguousarray(np.asarray(inputs["L0_ffn_gate_w"]).T),
        "upT": np.ascontiguousarray(np.asarray(inputs["L0_ffn_up_w"]).T),
        "dwT": np.ascontiguousarray(np.asarray(inputs["L0_ffn_down_w"]).T),
        "wmod": np.ascontiguousarray(np.asarray(inputs["L0_wmod"])),
        "gatepp": pp(inputs["L0_attn_gate"]),
        "lngpp": pp(inputs["L0_attn_ln_gamma"]),
        "lnbpp": pp(inputs["L0_attn_ln_beta"]),
        "td": np.ascontiguousarray(np.asarray(inputs["L0_attn_time_decay"]).reshape(S, 1)),
        "tf": np.ascontiguousarray(np.asarray(inputs["L0_attn_time_first"]).reshape(S, 1)),
    }
    cmat = np.asarray(inputs["L0_c"])
    in_maps = []
    for core in range(8):
        bi, h = core // 2, core % 2
        xb = np.roll(x[bi], -TO * h, axis=0)
        m = dict(shared)
        m["xT"] = np.ascontiguousarray(xb.T)
        m["x_tm"] = np.ascontiguousarray(xb[:TO])
        m["cpp"] = pp(cmat[bi])
        m["cmask"] = np.full((S, 1), float(h), np.float32)
        in_maps.append(m)
    return in_maps


_PROG_CACHE = {}


def kernel(**inputs):
    c = Cfg()
    if "prog" not in _PROG_CACHE:
        _PROG_CACHE["prog"] = build_program(c)
    nc = _PROG_CACHE["prog"]
    in_maps = stage_inputs(c, inputs)
    res = run_bass_kernel_spmd(nc, in_maps, core_ids=list(range(8)))
    out = np.empty((4, c.T, c.D), np.float32)
    for core in range(8):
        bi, h = core // 2, core % 2
        out[bi, h * c.TO:(h + 1) * c.TO, :] = res.results[core]["out_tm"]
    return out


# version 10
# speedup vs baseline: 2.0431x; 2.0431x over previous
import contextlib

import numpy as np

import concourse.bass as bass
import concourse.mybir as mybir
from concourse.bass_utils import run_bass_kernel_spmd

F32 = mybir.dt.float32
BF16 = mybir.dt.bfloat16
ALU = mybir.AluOpType
ACT = mybir.ActivationFunctionType
AXX = mybir.AxisListType.X

LN_EPS = 1e-5
LN2 = 0.6931471805599453


class Cfg:
    def __init__(self, D=2048, T=2048, TT=512, S=64, I=8192, debug=False):
        self.D, self.T, self.TT, self.S, self.I = D, T, TT, S, I
        self.NK = D // 128
        self.TO = T // 2
        self.NTT = T // TT
        self.NTC = self.TO // 128
        self.NI = I // 128
        self.NQ = self.NI // 4
        self.JM = 6 * D
        self.NJT = self.JM // TT
        self.NJ2 = self.JM // 128
        self.debug = debug


class B:

    ENGINES = ("sync", "tensor", "vector", "scalar", "gpsimd")

    def __init__(self, nc):
        self.nc = nc
        self.sems = {}
        self.cnt = {}
        self.waited = {}
        self.phase = []

    def make_sems(self, stack, names):
        for n in names:
            self.sems[n] = stack.enter_context(self.nc.semaphore(n))
            self.cnt[n] = 0

    def need(self, engine, deps):
        for s, v in deps:
            if v is None or v <= 0:
                continue
            key = (engine, s)
            if self.waited.get(key, 0) >= v:
                continue
            self.waited[key] = v
            sem = self.sems[s]
            self.phase.append((engine, lambda e, sem=sem, v=v: e.wait_ge(sem, v)))

    def op(self, engine, fn, deps=(), inc=None, inc_by=1, then_by=None):
        deps = list(deps)
        if engine == "vector" and self.cnt.get("V", 0) > 0:
            deps.append(("V", self.cnt["V"]))
        if engine == "scalar" and self.cnt.get("SC", 0) > 0:
            deps.append(("SC", self.cnt["SC"]))
        if engine == "gpsimd" and self.cnt.get("G", 0) > 0:
            deps.append(("G", self.cnt["G"]))
        self.need(engine, deps)
        if inc is None:
            self.phase.append((engine, fn))
            return None
        sem = self.sems[inc]
        self.cnt[inc] += inc_by
        val = self.cnt[inc]
        tn = inc_by if then_by is None else then_by
        self.phase.append(
            (engine, lambda e, fn=fn, sem=sem, n=tn: fn(e).then_inc(sem, n)))
        return val

    def run_phase(self, name):
        sched, self.phase = self.phase, []
        if not sched:
            return
        with self.nc.Block(name) as block:
            for eng in self.ENGINES:
                items = [f for e, f in sched if e == eng]
                if items:
                    getattr(block, eng)(
                        lambda engine, items=items: [f(engine) for f in items])


def build_program(c: Cfg):
    nc = bass.Bass("TRN2", target_bir_lowering=False)
    D, T, TT, S, NK, TO, NTT, NTC = c.D, c.T, c.TT, c.S, c.NK, c.TO, c.NTT, c.NTC

    xT = nc.dram_tensor("xT", [D, T], F32, kind="ExternalInput").ap()
    x_tm_h = nc.dram_tensor("x_tm", [TO, D], F32, kind="ExternalInput").ap()
    wsh_pk = nc.dram_tensor("wsh_pk", [NK * 128, NK * 128], F32, kind="ExternalInput").ap()
    wk_pk = nc.dram_tensor("wk_pk", [128, NK * S], F32, kind="ExternalInput").ap()
    wv_pk = nc.dram_tensor("wv_pk", [128, NK * S], F32, kind="ExternalInput").ap()
    woutT = nc.dram_tensor("woutT", [S, D], F32, kind="ExternalInput").ap()
    gw_r = nc.dram_tensor("gw_r", [c.I, D], F32, kind="ExternalInput").ap()
    up_r = nc.dram_tensor("up_r", [c.I, D], F32, kind="ExternalInput").ap()
    dw_r = nc.dram_tensor("dw_r", [D, c.I], F32, kind="ExternalInput").ap()
    gwT_h = nc.dram_tensor("gwT", [D, c.I], F32, kind="ExternalInput").ap()
    upT_h = nc.dram_tensor("upT", [D, c.I], F32, kind="ExternalInput").ap()
    dwT_h = nc.dram_tensor("dwT", [c.I, D], F32, kind="ExternalInput").ap()
    wmod_h = nc.dram_tensor("wmod", [D, c.JM], F32, kind="ExternalInput").ap()
    gatepp = nc.dram_tensor("gatepp", [128, NK], F32, kind="ExternalInput").ap()
    lngpp = nc.dram_tensor("lngpp", [128, NK], F32, kind="ExternalInput").ap()
    lnbpp = nc.dram_tensor("lnbpp", [128, NK], F32, kind="ExternalInput").ap()
    cpp = nc.dram_tensor("cpp", [128, NK], F32, kind="ExternalInput").ap()
    td_h = nc.dram_tensor("td", [S, 1], F32, kind="ExternalInput").ap()
    tf_h = nc.dram_tensor("tf", [S, 1], F32, kind="ExternalInput").ap()
    cmask_h = nc.dram_tensor("cmask", [S, 1], F32, kind="ExternalInput").ap()
    out_tm_h = nc.dram_tensor("out_tm", [TO, D], F32, kind="ExternalOutput").ap()
    dbg = {}
    if c.debug:
        for nm, shp, dt in [
            ("dbg_hb", [128, NK * T], BF16),
            ("dbg_states", [S, TO], F32),
            ("dbg_xmid", [128, NTC * D], F32),
            ("dbg_x2f", [128, NTC * D], BF16),
            ("dbg_h0", [128, TT], BF16),
            ("dbg_h2t0", [128, 2048], F32),
            ("dbg_dqx0", [128, 2048], BF16),
            ("dbg_hq0", [128, TT], BF16),
        ]:
            dbg[nm] = nc.dram_tensor(nm, shp, dt, kind="ExternalOutput").ap()

    b = B(nc)
    st_top = contextlib.ExitStack()
    st_mix = contextlib.ExitStack()
    st_p4 = contextlib.ExitStack()
    st_hb = contextlib.ExitStack()
    st_xs = contextlib.ExitStack()
    st_xm = contextlib.ExitStack()
    with st_top:
        b.make_sems(st_top, ["V", "SC", "G", "P", "Q", "GA", "GB", "GC", "GD",
                             "HX", "HY", "HO", "HO0", "HO1", "HLW", "HLD",
                             "RL0", "RL1", "FL0", "FL1", "TS0", "TS1",
                             "TX0", "TX1", "TH0", "TH1"])
        sb = lambda st, name, shp, dt, side=None: st.enter_context(
            nc.sbuf_tensor(name, shp, dt, side=side) if side else
            nc.sbuf_tensor(name, shp, dt))
        ps = [st_top.enter_context(nc.psum_tensor(f"ps{i}", [128, TT], F32))
              for i in range(8)]
        bank_free = [("Q", 0)] * 8
        bank_ready = [0] * 8

        g_sig = sb(st_top, "g_sig", [128, NK], F32)
        one_m_g = sb(st_top, "one_m_g", [128, NK], F32)
        lng = sb(st_top, "lng", [128, NK], F32)
        lnb = sb(st_top, "lnb", [128, NK], F32)
        u_neg = sb(st_top, "u_neg", [S, 1], F32)
        w_dec = sb(st_top, "w_dec", [S, 1], F32)
        cmask = sb(st_top, "cmask_s", [S, 1], F32)
        modpp = sb(st_top, "modpp", [128, 6 * NK], F32)
        ca1 = sb(st_top, "ca1", [128, NK], F32)
        ones_bf = sb(st_top, "ones_bf", [128, 1], BF16)
        qguard = sb(st_top, "qguard", [128, 1], F32)
        ones16 = sb(st_top, "ones16", [128, 1], F32)
        onesr = sb(st_top, "onesr", [1, 128], F32)
        ga_tm = sb(st_p4, "ga_tm", [128, D], BF16, side="right")
        sb_tm = sb(st_p4, "sb_tm", [128, D], BF16, side="right")
        cb1_tm = sb(st_p4, "cb1_tm", [128, D], F32, side="right")
        gb_tm = sb(st_top, "gb_tm", [128, D], BF16)
        stat = sb(st_mix, "stat", [1, 6 * TT], F32, side="right")
        MU, M2, MSQ, VAR, SD, RINV = range(6)
        stv = lambda i: stat[0:1, i * TT:(i + 1) * TT]
        sq_scr = [sb(st_mix, f"sq{i}", [128, TT], BF16, side="right") for i in range(2)]
        f_scr = [sb(st_mix, f"fs{i}", [128, TT], F32, side="right") for i in range(4)]
        states_bf = sb(st_mix, "states_bf", [S, TO], BF16, side="right")

        with contextlib.ExitStack() as ph:
            gpv = sb(ph, "gpv", [128, NK], F32)
            tdv = sb(ph, "tdv", [S, 1], F32)
            tfv = sb(ph, "tfv", [S, 1], F32)
            cvv = sb(ph, "cvv", [128, NK], F32)
            silu_c = sb(ph, "silu_c", [128, NK], F32)
            silu_cb = sb(ph, "silu_cb", [128, NK], BF16)
            diag = sb(ph, "diag", [128, NK * NK], BF16)
            iden = sb(ph, "iden", [128, 128], F32)
            wmbuf = [sb(ph, f"wmbuf{i}", [128, 8 * TT], BF16) for i in range(2)]
            stackb = sb(ph, "stackb", [NK, c.JM], F32)
            mrow = sb(ph, "mrow", [1, c.JM], F32)
            mtmp = sb(ph, "mtmp", [c.NJ2, 128], F32)

            for dst, src in [(gpv, gatepp), (tdv, td_h), (tfv, tf_h), (cvv, cpp),
                             (cmask, cmask_h), (lng, lngpp), (lnb, lnbpp)]:
                b.op("gpsimd", lambda e, d=dst, s=src: e.dma_start(out=d[:, :], in_=s[:, :]),
                     inc="GC", inc_by=16)
            v_gc = b.cnt["GC"]
            for t_, v_ in [(ones_bf, 1.0), (ones16, 1.0), (onesr, 1.0),
                           (qguard, 1e-35)]:
                b.op("gpsimd", lambda e, t=t_, v=v_: e.memset(t[:, :], v), inc="Q")
            b.op("gpsimd", lambda e: e.memset(diag[:, :], 0.0), inc="Q")
            b.op("gpsimd", lambda e: e.iota(iden[:, :], [[1, 128]], channel_multiplier=-1,
                                            allow_small_or_imprecise_dtypes=True), inc="Q")
            v_q = b.cnt["Q"]
            v_id = b.op("vector", lambda e: e.tensor_scalar(
                out=iden[:, :], in0=iden[:, :], scalar1=0.0, scalar2=None,
                op0=ALU.is_equal), deps=[("Q", v_q)], inc="V")
            v_sig = b.op("scalar", lambda e: e.activation(out=g_sig[:, :], in_=gpv[:, :],
                         func=ACT.Sigmoid), deps=[("GC", v_gc)], inc="SC")
            b.op("vector", lambda e: e.tensor_scalar(
                out=one_m_g[:, :], in0=g_sig[:, :], scalar1=-1.0, scalar2=1.0,
                op0=ALU.mult, op1=ALU.add), deps=[("SC", v_sig)], inc="V")
            v_u = b.op("scalar", lambda e: e.activation(out=u_neg[:, :], in_=tfv[:, :],
                       func=ACT.Exp), deps=[("GC", v_gc)], inc="SC")
            b.op("vector", lambda e: e.tensor_scalar(
                out=u_neg[:, :], in0=u_neg[:, :], scalar1=-1.0, scalar2=None,
                op0=ALU.mult), deps=[("SC", v_u)], inc="V")
            b.op("scalar", lambda e: e.activation(out=w_dec[:, :], in_=tdv[:, :],
                 func=ACT.Exp), deps=[("GC", v_gc)], inc="SC")
            v_silu = b.op("scalar", lambda e: e.activation(out=silu_c[:, :], in_=cvv[:, :],
                          func=ACT.Silu), deps=[("GC", v_gc)], inc="SC")
            v_scb = b.op("vector", lambda e: e.tensor_copy(out=silu_cb[:, :], in_=silu_c[:, :]),
                         deps=[("SC", v_silu)], inc="V")
            for k in range(NK):
                b.op("vector", lambda e, k=k: e.tensor_copy(
                    out=diag[:, k * NK + k:k * NK + k + 1], in_=silu_cb[:, k:k + 1]),
                    deps=[("V", v_scb), ("Q", v_q)], inc="V")
            v_diag = b.cnt["V"]

            gen_done_P = {}
            kgen = 0
            j0 = 0
            while j0 < c.NJT:
                wave = min(8, c.NJT - j0)
                for k in range(NK):
                    buf = wmbuf[kgen % 2]
                    gsem = "GA" if kgen % 2 == 0 else "GB"
                    vdma = b.op("gpsimd", lambda e, k=k, j0=j0, w=wave, buf=buf:
                                e.dma_start(out=buf[:, 0:w * TT],
                                            in_=wmod_h[k * 128:(k + 1) * 128,
                                                       j0 * TT:(j0 + w) * TT]),
                                deps=[("P", gen_done_P.get(kgen % 2, 0))],
                                inc=gsem, inc_by=16)
                    mmdeps = [(gsem, vdma), ("V", v_diag)]
                    if k == 0:
                        for j in range(wave):
                            mmdeps.append(bank_free[j])
                    vmm = None
                    for j in range(wave):
                        vmm = b.op("tensor", lambda e, k=k, j=j, j0=j0, buf=buf: e.matmul(
                            ps[j][0:NK, 0:TT], diag[:, k * NK:(k + 1) * NK],
                            buf[:, j * TT:(j + 1) * TT],
                            start=(k == 0), stop=(k == NK - 1)),
                            deps=mmdeps, inc="P")
                        mmdeps = []
                    gen_done_P[kgen % 2] = vmm
                    kgen += 1
                for j in range(wave):
                    bank_ready[j] = b.cnt["P"]
                    vev = b.op("vector", lambda e, j=j, j0=j0: e.tensor_copy(
                        out=stackb[0:NK, (j0 + j) * TT:(j0 + j + 1) * TT],
                        in_=ps[j][0:NK, 0:TT]),
                        deps=[("P", bank_ready[j])], inc="V")
                    bank_free[j] = ("V", vev)
                j0 += wave
            for j in range(c.NJT):
                bank = 6 if j % 2 == 0 else 7
                vmm = b.op("tensor", lambda e, j=j, bank=bank: e.matmul(
                    ps[bank][0:1, 0:TT], ones16[0:NK, 0:1],
                    stackb[0:NK, j * TT:(j + 1) * TT], start=True, stop=True),
                    deps=[bank_free[bank], ("V", b.cnt["V"])], inc="P")
                bank_ready[bank] = vmm
                vev = b.op("vector", lambda e, j=j, bank=bank: e.tensor_copy(
                    out=mrow[0:1, j * TT:(j + 1) * TT], in_=ps[bank][0:1, 0:TT]),
                    deps=[("P", vmm)], inc="V")
                bank_free[bank] = ("V", vev)
            v_mrow = b.cnt["V"]
            v_rel = b.op("gpsimd", lambda e: e.dma_start(
                out=mtmp[:, :],
                in_=mrow[0:1, :].rearrange("p (a x) -> p a x", x=128)),
                deps=[("V", v_mrow)], inc="GC", inc_by=16)
            v_tr = b.op("tensor", lambda e: e.transpose(
                ps[0][:, 0:c.NJ2], mtmp[:, :], iden[0:c.NJ2, 0:c.NJ2]),
                deps=[("GC", v_rel), ("V", v_id), bank_free[0]], inc="P")
            bank_ready[0] = v_tr
            v_mpp = b.op("scalar", lambda e: e.copy(out=modpp[:, :], in_=ps[0][:, 0:c.NJ2]),
                         deps=[("P", v_tr)], inc="SC")
            bank_free[0] = ("SC", v_mpp)
            b.op("vector", lambda e: e.tensor_scalar(
                out=ca1[:, :], in0=modpp[:, NK:2 * NK], scalar1=1.0, scalar2=None,
                op0=ALU.add), deps=[("SC", v_mpp)], inc="V")

            def emit_tm_bcast(slot, dst, mode, bankpair):
                for j in range(D // TT):
                    bank = bankpair[j % 2]
                    vmm = b.op("tensor", lambda e, j=j, slot=slot, bank=bank: e.matmul(
                        ps[bank][:, 0:TT], onesr[:, :],
                        mrow[0:1, slot * D + j * TT:slot * D + (j + 1) * TT],
                        start=True, stop=True),
                        deps=[("V", v_mrow), bank_free[bank]], inc="P")
                    bank_ready[bank] = vmm
                    dsl = lambda dst=dst, j=j: dst[:, j * TT:(j + 1) * TT]
                    if mode == "signmax":
                        scr = f_scr[j % 2]
                        scr2 = f_scr[2 + (j % 2)]
                        vge = b.op("vector", lambda e, bank=bank, scr=scr: e.tensor_scalar(
                            out=scr[:, :], in0=ps[bank][:, 0:TT], scalar1=0.0,
                            scalar2=None, op0=ALU.is_ge), deps=[("P", vmm)], inc="V")
                        vsg = b.op("vector", lambda e, scr=scr: e.tensor_scalar(
                            out=scr[:, :], in0=scr[:, :], scalar1=2.0, scalar2=-1.0,
                            op0=ALU.mult, op1=ALU.add), inc="V")
                        vab0 = b.op("scalar", lambda e, bank=bank, scr2=scr2:
                                    e.activation(out=scr2[:, :], in_=ps[bank][:, 0:TT],
                                                 func=ACT.Abs),
                                    deps=[("P", vmm), ("V", b.cnt["V"])], inc="SC")
                        vab = b.op("vector", lambda e, scr2=scr2: e.tensor_scalar(
                            out=scr2[:, :], in0=scr2[:, :], scalar1=0.25,
                            scalar2=None, op0=ALU.max), deps=[("SC", vab0)], inc="V")
                        vf = b.op("vector", lambda e, scr=scr, scr2=scr2, dsl=dsl:
                                  e.tensor_tensor(out=dsl(), in0=scr[:, :],
                                                  in1=scr2[:, :], op=ALU.mult), inc="V")
                        bank_free[bank] = ("V", vf)
                    elif mode == "add1":
                        vf = b.op("vector", lambda e, bank=bank, dsl=dsl: e.tensor_scalar(
                            out=dsl(), in0=ps[bank][:, 0:TT], scalar1=1.0,
                            scalar2=None, op0=ALU.add), deps=[("P", vmm)], inc="V")
                        bank_free[bank] = ("V", vf)
                    else:
                        vf = b.op("vector", lambda e, bank=bank, dsl=dsl: e.tensor_copy(
                            out=dsl(), in_=ps[bank][:, 0:TT]), deps=[("P", vmm)], inc="V")
                        bank_free[bank] = ("V", vf)

            emit_tm_bcast(2, ga_tm, "signmax", (0, 1))
            emit_tm_bcast(5, gb_tm, "signmax", (2, 3))
            emit_tm_bcast(4, cb1_tm, "add1", (4, 5))
            emit_tm_bcast(3, sb_tm, "copy", (6, 7))
            b.run_phase("p0_mod")

        def ln_stats_and_bcast(get_xbf, extra_dep, banks, n_chunks=NK):
            bSX, bSX2, bBM, bBR = banks
            sqdep = [0, 0]
            dep0 = list(extra_dep)
            for k in range(n_chunks):
                scr = sq_scr[k % 2]
                vsq = b.op("scalar", lambda e, k=k, scr=scr: e.activation(
                    out=scr[:, :], in_=get_xbf(k), func=ACT.Square),
                    deps=dep0 + [("P", sqdep[k % 2])], inc="SC")
                mmdeps = dep0 + [("SC", vsq)]
                if k == 0:
                    mmdeps += [bank_free[bSX], bank_free[bSX2]]
                b.op("tensor", lambda e, k=k, first=(k == 0), last=(k == n_chunks - 1):
                     e.matmul(ps[bSX][0:1, 0:TT], ones_bf[:, :], get_xbf(k),
                              start=first, stop=last), deps=mmdeps, inc="P")
                vmm2 = b.op("tensor", lambda e, k=k, scr=scr,
                            first=(k == 0), last=(k == n_chunks - 1):
                            e.matmul(ps[bSX2][0:1, 0:TT], ones_bf[:, :], scr[:, :],
                                     start=first, stop=last), inc="P")
                sqdep[k % 2] = vmm2
            bank_ready[bSX] = bank_ready[bSX2] = b.cnt["P"]
            pst = b.cnt["P"]
            v1 = b.op("vector", lambda e: e.tensor_scalar(
                out=stv(MU), in0=ps[bSX][0:1, 0:TT], scalar1=1.0 / D, scalar2=None,
                op0=ALU.mult), deps=[("P", pst)], inc="V")
            v2 = b.op("vector", lambda e: e.tensor_scalar(
                out=stv(M2), in0=ps[bSX2][0:1, 0:TT], scalar1=1.0 / D, scalar2=None,
                op0=ALU.mult), deps=[("P", pst)], inc="V")
            bank_free[bSX] = ("V", v1)
            bank_free[bSX2] = ("V", v2)
            v3 = b.op("vector", lambda e: e.tensor_tensor(
                out=stv(MSQ), in0=stv(MU), in1=stv(MU), op=ALU.mult),
                deps=[("V", v1)], inc="V")
            v4 = b.op("vector", lambda e: e.scalar_tensor_tensor(
                out=stv(VAR), in0=stv(M2), scalar=LN_EPS, in1=stv(MSQ),
                op0=ALU.add, op1=ALU.subtract), deps=[("V", v3)], inc="V")
            v5 = b.op("scalar", lambda e: e.activation(
                out=stv(SD), in_=stv(VAR), func=ACT.Sqrt),
                deps=[("V", v4)], inc="SC")
            v6 = b.op("vector", lambda e: e.reciprocal(out=stv(RINV), in_=stv(SD)),
                      deps=[("SC", v5)], inc="V")
            vbm = b.op("tensor", lambda e: e.matmul(
                ps[bBM][:, 0:TT], onesr[:, :], stv(MU), start=True, stop=True),
                deps=[("V", v1), bank_free[bBM]], inc="P")
            vbr = b.op("tensor", lambda e: e.matmul(
                ps[bBR][:, 0:TT], onesr[:, :], stv(RINV), start=True, stop=True),
                deps=[("V", v6), bank_free[bBR]], inc="P")
            bank_ready[bBM], bank_ready[bBR] = vbm, vbr
            return bBM, bBR, vbm, vbr

        def ln_apply(x_ap, out_ap, bBM, bBR, pdep, sc_pp, bias_pp, scr):
            va = b.op("vector", lambda e: e.tensor_tensor(
                out=scr[:, :], in0=x_ap, in1=ps[bBM][:, 0:TT], op=ALU.subtract),
                deps=pdep, inc="V")
            vb = b.op("vector", lambda e: e.tensor_tensor(
                out=scr[:, :], in0=scr[:, :], in1=ps[bBR][:, 0:TT], op=ALU.mult),
                deps=[("V", va)], inc="V")
            vc = b.op("vector", lambda e: e.tensor_scalar(
                out=out_ap, in0=scr[:, :], scalar1=sc_pp, scalar2=bias_pp,
                op0=ALU.mult, op1=ALU.add), deps=[("V", vb)], inc="V")
            return vc

        hb = sb(st_hb, "hb", [128, NK * T], BF16)
        with contextlib.ExitStack() as ph:
            xb = sb(ph, "xb", [128, NK * T], BF16)
            for k in range(NK):
                b.op("gpsimd", lambda e, k=k: e.dma_start(
                    out=xb[:, k * T:(k + 1) * T], in_=xT[k * 128:(k + 1) * 128, :]),
                    inc="GC", inc_by=16)
            v_xb = b.cnt["GC"]
            for ti in range(NTT):
                banks = (0, 1, 2, 3) if ti % 2 == 0 else (4, 5, 6, 7)
                getx = lambda k, ti=ti: xb[:, k * T + ti * TT:k * T + ti * TT + TT]
                bBM, bBR, vbm, vbr = ln_stats_and_bcast(getx, [("GC", v_xb)], banks)
                lastap = 0
                for k in range(NK):
                    lastap = ln_apply(
                        getx(k), hb[:, k * T + ti * TT:k * T + ti * TT + TT],
                        bBM, bBR, [("P", max(vbm, vbr))],
                        ca1[:, k:k + 1], modpp[:, 0 * NK + k:0 * NK + k + 1],
                        f_scr[k % 2])
                bank_free[bBM] = bank_free[bBR] = ("V", lastap)
            if c.debug:
                vv = b.cnt["V"]
                b.op("sync", lambda e: e.dma_start(out=dbg["dbg_hb"][:, :], in_=hb[:, :]),
                     deps=[("V", vv)], inc="HO", inc_by=16)
                b.need("sync", [("HO", b.cnt["HO"])])
            b.run_phase("p1_h")

        x_shift = sb(st_xs, "x_shift", [128, NK * T], BF16, side="right")
        with contextlib.ExitStack() as ph:
            wsb = [sb(ph, f"wsb{i}", [128, NK * 128], BF16) for i in range(2)]
            gen_done = {0: 0, 1: 0}
            for e_i in range(NK):
                buf = wsb[e_i % 2]
                gsem = "GA" if e_i % 2 == 0 else "GB"
                vdma = b.op("gpsimd", lambda e, e_i=e_i, buf=buf: e.dma_start(
                    out=buf[:, :],
                    in_=wsh_pk[e_i * 128:(e_i + 1) * 128, :]),
                    deps=[("P", gen_done[e_i % 2])], inc=gsem, inc_by=16)
                for ti in range(NTT):
                    bank = (e_i * NTT + ti) % 2
                    src_t = ((ti + NTT // 2) % NTT) * TT
                    mmdeps = [(gsem, vdma), bank_free[bank]]
                    vmm = None
                    for k in range(NK):
                        vmm = b.op("tensor", lambda e, k=k, e_i=e_i, buf=buf, bank=bank,
                                   src_t=src_t, first=(k == 0), last=(k == NK - 1):
                                   e.matmul(ps[bank][:, 0:TT],
                                            buf[:, k * 128:(k + 1) * 128],
                                            hb[:, k * T + src_t:k * T + src_t + TT],
                                            start=first, stop=last),
                                   deps=mmdeps, inc="P")
                        mmdeps = []
                    bank_ready[bank] = vmm
                    gen_done[e_i % 2] = vmm
                    scr = f_scr[2 + (ti % 2)]
                    vh = b.op("vector", lambda e, e_i=e_i, ti=ti, scr=scr: e.tensor_scalar(
                        out=scr[:, :], in0=hb[:, e_i * T + ti * TT:e_i * T + ti * TT + TT],
                        scalar1=one_m_g[:, e_i:e_i + 1], scalar2=None, op0=ALU.mult),
                        inc="V")
                    vev = b.op("vector", lambda e, e_i=e_i, ti=ti, bank=bank, scr=scr:
                               e.scalar_tensor_tensor(
                                   out=x_shift[:, e_i * T + ti * TT:e_i * T + ti * TT + TT],
                                   in0=ps[bank][:, 0:TT], scalar=g_sig[:, e_i:e_i + 1],
                                   in1=scr[:, :], op0=ALU.mult, op1=ALU.add),
                               deps=[("P", vmm), ("V", vh)], inc="V")
                    bank_free[bank] = ("V", vev)
            b.run_phase("p2_shift")
        st_hb.close()

        with contextlib.ExitStack() as ph:
            kv_all = sb(ph, "kv_all", [S, 2 * T], F32)
            wkv = sb(ph, "wkv", [S, T], F32)
            wk_sb = sb(ph, "wk_sb", [128, NK * S], BF16)
            wv_sb = sb(ph, "wv_sb", [128, NK * S], BF16)
            xn = sb(ph, "xn", [128, NK * TT], BF16)
            b.op("gpsimd", lambda e: e.dma_start(out=wk_sb[:, :], in_=wk_pk[:, :]),
                 inc="GC", inc_by=16)
            b.op("gpsimd", lambda e: e.dma_start(out=wv_sb[:, :], in_=wv_pk[:, :]),
                 inc="GC", inc_by=16)
            v_wkv = b.cnt["GC"]
            for ti in range(NTT):
                getx = lambda k, ti=ti: x_shift[:, k * T + ti * TT:k * T + ti * TT + TT]
                bBM, bBR, vbm, vbr = ln_stats_and_bcast(getx, [], (0, 1, 2, 3))
                lastap = 0
                for k in range(NK):
                    lastap = ln_apply(
                        getx(k), xn[:, k * TT:(k + 1) * TT], bBM, bBR,
                        [("P", max(vbm, vbr))],
                        lng[:, k:k + 1], lnb[:, k:k + 1], f_scr[k % 2])
                bank_free[bBM] = bank_free[bBR] = ("V", lastap)
                for out_i, w_t in [(0, wk_sb), (1, wv_sb)]:
                    bank = 4 + out_i
                    mmdeps = [("GC", v_wkv), ("V", lastap), bank_free[bank]]
                    vmm = None
                    for k in range(NK):
                        vmm = b.op("tensor", lambda e, k=k, w_t=w_t, bank=bank,
                                   first=(k == 0), last=(k == NK - 1):
                                   e.matmul(ps[bank][0:S, 0:TT],
                                            w_t[:, k * S:(k + 1) * S],
                                            xn[:, k * TT:(k + 1) * TT],
                                            start=first, stop=last),
                                   deps=mmdeps, inc="P")
                        mmdeps = []
                    bank_ready[bank] = vmm
                    vev = b.op("vector", lambda e, ti=ti, out_i=out_i, bank=bank:
                               e.tensor_copy(
                                   out=kv_all[:, out_i * T + ti * TT:out_i * T + ti * TT + TT],
                                   in_=ps[bank][0:S, 0:TT]),
                               deps=[("P", vmm)], inc="V")
                    bank_free[bank] = ("V", vev)
            v_kv = b.cnt["V"]
            vsg = b.op("scalar", lambda e: e.activation(
                out=wkv[:, :], in_=kv_all[:, 0:T], func=ACT.Sigmoid),
                deps=[("V", v_kv)], inc="SC")
            vts = b.op("vector", lambda e: e.tensor_scalar(
                out=wkv[:, :], in0=wkv[:, :], scalar1=u_neg[:, :], scalar2=None,
                op0=ALU.mult), deps=[("SC", vsg)], inc="V")
            vex = b.op("scalar", lambda e: e.activation(
                out=wkv[:, :], in_=wkv[:, :], func=ACT.Exp),
                deps=[("V", vts)], inc="SC")
            v_wkvd = b.op("vector", lambda e: e.tensor_tensor(
                out=wkv[:, :], in0=wkv[:, :], in1=kv_all[:, T:2 * T], op=ALU.mult),
                deps=[("SC", vex)], inc="V")
            wrep = sb(ph, "wrep", [S, T], F32)
            scanb = sb(ph, "scanb", [S, TO], F32)
            carry = sb(ph, "carry", [S, 1], F32)
            states = sb(ph, "states", [S, TO], F32)
            vwr = b.op("vector", lambda e: e.tensor_scalar(
                out=wrep[:, :], in0=wkv[:, :], scalar1=0.0, scalar2=w_dec[:, :],
                op0=ALU.mult, op1=ALU.add), deps=[("V", v_wkvd)], inc="V")
            vs1 = b.op("vector", lambda e: e.tensor_tensor_scan(
                out=scanb[:, :], data0=wrep[:, TO:T], data1=wkv[:, TO:T],
                initial=0.0, op0=ALU.mult, op1=ALU.add),
                deps=[("V", vwr)], inc="V")
            vc1 = b.op("vector", lambda e: e.tensor_tensor(
                out=carry[:, :], in0=scanb[:, TO - 1:TO], in1=cmask[:, :],
                op=ALU.mult), deps=[("V", vs1)], inc="V")
            vs2 = b.op("vector", lambda e: e.tensor_tensor_scan(
                out=states[:, :], data0=wrep[:, 0:TO], data1=wkv[:, 0:TO],
                initial=carry[:, :], op0=ALU.mult, op1=ALU.add),
                deps=[("V", vc1)], inc="V")
            v_stb = b.op("vector", lambda e: e.tensor_copy(
                out=states_bf[:, :], in_=states[:, :]), deps=[("V", vs2)], inc="V")
            if c.debug:
                b.op("sync", lambda e: e.dma_start(out=dbg["dbg_states"][:, :],
                     in_=states[:, :]), deps=[("V", vs2)], inc="HO", inc_by=16)
                b.need("sync", [("HO", b.cnt["HO"])])
            b.run_phase("p3a_kv")
        st_xs.close()

        xmid = sb(st_xm, "xmid_tm", [128, NTC * D], F32)
        with contextlib.ExitStack() as ph:
            wo_sb = sb(ph, "wo_sb", [S, D], BF16)
            xo = [sb(ph, f"xo{i}", [128, D], F32) for i in range(2)]
            b.op("gpsimd", lambda e: e.dma_start(out=wo_sb[:, :], in_=woutT[:, :]),
                 inc="GC", inc_by=16)
            v_wo = b.cnt["GC"]
            xo_done = {0: 0, 1: 0}
            scr_free = {0: ("Q", 0), 1: ("Q", 0), 2: ("Q", 0), 3: ("Q", 0)}
            for tc in range(NTC):
                buf = xo[tc % 2]
                hxs = "HX" if tc % 2 == 0 else "HY"
                vdma = b.op("sync", lambda e, tc=tc, buf=buf: e.dma_start(
                    out=buf[:, :], in_=x_tm_h[tc * 128:(tc + 1) * 128, :]),
                    deps=[("V", xo_done[tc % 2])], inc=hxs, inc_by=16)
                for dt in range(4):
                    bank = (tc * 4 + dt) % 4
                    vmm = b.op("tensor", lambda e, tc=tc, dt=dt, bank=bank: e.matmul(
                        ps[bank][:, 0:TT], states_bf[0:S, tc * 128:(tc + 1) * 128],
                        wo_sb[0:S, dt * TT:(dt + 1) * TT], start=True, stop=True),
                        deps=[("GC", v_wo), bank_free[bank]], inc="P")
                    bank_ready[bank] = vmm
                    scr = f_scr[dt % 4]
                    vmu = b.op("vector", lambda e, dt=dt, bank=bank, scr=scr:
                               e.tensor_tensor(
                                   out=scr[:, :], in0=ps[bank][:, 0:TT],
                                   in1=ga_tm[:, dt * TT:(dt + 1) * TT], op=ALU.mult),
                               deps=[("P", vmm), scr_free[dt]], inc="V")
                    vad = b.op("vector", lambda e, tc=tc, dt=dt, scr=scr, buf=buf:
                               e.tensor_tensor(
                                   out=xmid[:, tc * D + dt * TT:tc * D + dt * TT + TT],
                                   in0=scr[:, :], in1=buf[:, dt * TT:(dt + 1) * TT],
                                   op=ALU.add),
                               deps=[("V", vmu), (hxs, vdma)], inc="V")
                    bank_free[bank] = ("V", vad)
                    scr_free[dt] = ("V", vad)
                    xo_done[tc % 2] = vad
            if c.debug:
                vv = b.cnt["V"]
                b.op("sync", lambda e: e.dma_start(out=dbg["dbg_xmid"][:, :],
                     in_=xmid[:, :]), deps=[("V", vv)], inc="HO", inc_by=16)
                b.need("sync", [("HO", b.cnt["HO"])])
            b.run_phase("p3b_res")
        st_mix.close()

        x2f = sb(st_xm, "x2f", [128, NK * NTC * 128], BF16)
        qguard2 = qguard
        c1275 = sb(st_xm, "c1275", [128, 1], F32)
        bstep = sb(st_xm, "bstep", [128, 1], F32)
        bistp = sb(st_xm, "bistp", [128, 1], F32)
        REPall = sb(st_xm, "REPall", [64, 16 * 128], BF16)
        QF = 512
        qsc = []
        for i in range(2):
            qsc.append(dict(
                A=sb(st_xm, f"qA{i}", [128, QF], F32),
                Bq=sb(st_xm, f"qB{i}", [128, QF], F32),
                Cq=sb(st_xm, f"qC{i}", [128, QF], F32),
                X=sb(st_xm, f"qX{i}", [128, QF], BF16),
                X2=sb(st_xm, f"qX2{i}", [128, QF], BF16),
                X3=sb(st_xm, f"qX3{i}", [128, QF], BF16),
                Xi=sb(st_xm, f"qXi{i}", [128, QF], mybir.dt.int16),
                Xc=sb(st_xm, f"qXc{i}", [128, QF], mybir.dt.int16),
                sm=sb(st_xm, f"qsm{i}", [128, 48], F32),
                busy=[{"V": 0, "SC": 0, "G": 0}],
            ))
        qset_rr = [0]
        b.op("gpsimd", lambda e: e.memset(c1275[:, :], 127.5), inc="Q")
        b.op("gpsimd", lambda e: e.memset(bstep[:, :], -131.0 * LN2), inc="Q")
        b.op("gpsimd", lambda e: e.memset(bistp[:, :], 131.0 * LN2), inc="Q")
        b.op("gpsimd", lambda e: e.iota(
            REPall[:, :].rearrange("k (hc a bb) -> k hc a bb", hc=16, a=4),
            [[4, 16], [1, 4], [0, 32]], channel_multiplier=-1,
            allow_small_or_imprecise_dtypes=True), inc="Q")
        v_q5 = b.cnt["Q"]
        v_rep = b.op("vector", lambda e: e.tensor_scalar(
            out=REPall[:, :], in0=REPall[:, :], scalar1=0.0, scalar2=None,
            op0=ALU.is_equal), deps=[("Q", v_q5)], inc="V")

        def chain_v3(src_ap, dst_ap, F, src_deps, inv_src, s_src, amax_into_sm):
            si = qset_rr[0]
            qset_rr[0] = 1 - si
            q = qsc[si]
            NB = F // 32
            A, Bq, Cq, X, X2, X3, Xi, Xc, sm = (
                q["A"], q["Bq"], q["Cq"], q["X"], q["X2"], q["X3"],
                q["Xi"], q["Xc"], q["sm"])
            bz = q["busy"][0]
            gate = [("V", bz["V"]), ("SC", bz["SC"]), ("G", bz["G"])]
            s3 = lambda t: t[:, 0:F].rearrange("p (b j) -> p b j", j=32)
            deps0 = list(src_deps) + gate
            if amax_into_sm:
                b.op("vector", lambda e: e.tensor_reduce(
                    out=sm[:, 0:NB], in_=s3(src_ap), axis=AXX, op=ALU.max,
                    apply_absolute_value=True), deps=deps0, inc="V")
                b.op("vector", lambda e: e.tensor_scalar(
                    out=sm[:, 16:16 + NB], in0=sm[:, 0:NB], scalar1=1.0 / 448.0,
                    scalar2=1e-30, op0=ALU.mult, op1=ALU.max), inc="V")
                v_inv = b.op("vector", lambda e: e.reciprocal(
                    out=sm[:, 32:32 + NB], in_=sm[:, 16:16 + NB]), inc="V")
                deps_y = deps0 + [("V", v_inv)]
            else:
                deps_y = deps0
            y_eng = "vector"
            y_sem = "V"
            g_y = b.op(y_eng, lambda e: e.tensor_tensor(
                out=(s3(A) if amax_into_sm else A[:, 0:F]),
                in0=(s3(src_ap) if amax_into_sm else src_ap),
                in1=inv_src(sm), op=ALU.mult), deps=deps_y, inc=y_sem)
            s_ya = b.op("scalar", lambda e: e.activation(
                out=X[:, 0:F], in_=A[:, 0:F], func=ACT.Abs,
                bias=qguard2[:, 0:1]), deps=[(y_sem, g_y)] + gate, inc="SC")
            s_l = b.op("scalar", lambda e: e.activation(
                out=Bq[:, 0:F], in_=X[:, 0:F], func=ACT.Ln), inc="SC")
            s_i1 = b.op("scalar", lambda e: e.activation(
                out=Xi[:, 0:F], in_=Bq[:, 0:F], func=ACT.Copy,
                scale=1.0 / LN2, bias=128.0), inc="SC")
            g_d1 = b.op("vector", lambda e: e.scalar_tensor_tensor(
                out=Cq[:, 0:F], in0=Bq[:, 0:F], scalar=1.0 / LN2,
                in1=Xi[:, 0:F], op0=ALU.mult, op1=ALU.subtract),
                deps=[("SC", s_i1)], inc="V")
            v_c1 = b.op("vector", lambda e: e.tensor_scalar(
                out=Xc[:, 0:F], in0=Cq[:, 0:F], scalar1=-128.0, scalar2=None,
                op0=ALU.is_lt), deps=[("V", g_d1)], inc="V")
            g_E = b.op("vector", lambda e: e.tensor_tensor(
                out=Xi[:, 0:F], in0=Xi[:, 0:F], in1=Xc[:, 0:F],
                op=ALU.subtract), deps=[("V", v_c1)], inc="V")
            s_st = b.op("scalar", lambda e: e.activation(
                out=X2[:, 0:F], in_=Xi[:, 0:F], func=ACT.Exp, scale=LN2,
                bias=bstep[:, 0:1]), deps=[("V", g_E)], inc="SC")
            v_is = b.op("scalar", lambda e: e.activation(
                out=X3[:, 0:F], in_=Xi[:, 0:F], func=ACT.Exp, scale=-LN2,
                bias=bistp[:, 0:1]), deps=[("SC", s_st)], inc="SC")
            v_is = ("SC", v_is)
            v_yq = b.op("vector", lambda e: e.tensor_tensor(
                out=Cq[:, 0:F], in0=A[:, 0:F], in1=X3[:, 0:F], op=ALU.mult),
                deps=[v_is, ("V", g_E)], inc="V")
            s_i2 = b.op("scalar", lambda e: e.activation(
                out=Xc[:, 0:F], in_=Cq[:, 0:F], func=ACT.Copy,
                bias=24.5), deps=[("V", v_yq)], inc="SC")
            g_d2 = b.op("vector", lambda e: e.scalar_tensor_tensor(
                out=A[:, 0:F], in0=Cq[:, 0:F], scalar=24.5,
                in1=Xc[:, 0:F], op0=ALU.add, op1=ALU.subtract),
                deps=[("SC", s_i2)], inc="V")
            v_c2 = b.op("vector", lambda e: e.tensor_scalar(
                out=Xi[:, 0:F], in0=A[:, 0:F], scalar1=0.0, scalar2=None,
                op0=ALU.is_lt), deps=[("V", g_d2)], inc="V")
            g_r = b.op("vector", lambda e: e.tensor_tensor(
                out=Xc[:, 0:F], in0=Xc[:, 0:F], in1=Xi[:, 0:F],
                op=ALU.subtract), deps=[("V", v_c2)], inc="V")
            g_q = b.op("vector", lambda e: e.scalar_tensor_tensor(
                out=Bq[:, 0:F], in0=Xc[:, 0:F], scalar=-24.0,
                in1=X2[:, 0:F], op0=ALU.add, op1=ALU.mult),
                deps=[("V", g_r)], inc="V")
            v_dq = b.op("vector", lambda e: e.tensor_tensor(
                out=(s3(dst_ap) if amax_into_sm else dst_ap),
                in0=(s3(Bq) if amax_into_sm else Bq[:, 0:F]),
                in1=s_src(sm), op=ALU.mult), deps=[("V", g_q)], inc="V")
            q["busy"][0] = {"V": v_dq, "SC": s_i2, "G": 0}
            return v_dq, [("V", b.cnt["V"]), (y_sem, g_y)]

        def emit_quant(src_ap, dst_ap, F, src_deps):
            NB = F // 32
            inv_src = lambda sm: (sm[:, 32:32 + NB].rearrange("p b -> p b ()")
                                  .broadcast_to((128, NB, 32)))
            s_src = lambda sm: (sm[:, 16:16 + NB].rearrange("p b -> p b ()")
                                .broadcast_to((128, NB, 32)))
            return chain_v3(src_ap, dst_ap, F, src_deps, inv_src, s_src, True)

        def emit_quant_ps(src_ap, dst_ap, F, src_deps, inv_ps, s_ps):
            return chain_v3(src_ap, dst_ap, F, src_deps,
                            lambda sm: inv_ps, lambda sm: s_ps, False)

        with contextlib.ExitStack() as ph:
            h2t = [sb(ph, f"h2t{i}", [128, D], F32) for i in range(2)]
            dqx = [sb(ph, f"dqx{i}", [128, D], BF16) for i in range(2)]
            sqf = sb(ph, "sqf", [128, D], F32)
            stt_s = sb(ph, "stt_s", [128, 8], F32)
            h2_free = {0: [("Q", 0)], 1: [("Q", 0)]}
            dq_free = {0: ("Q", 0), 1: ("Q", 0)}
            for tc in range(NTC):
                xsl = lambda tc=tc: xmid[:, tc * D:(tc + 1) * D]
                v_s1 = b.op("vector", lambda e, xsl=xsl: e.tensor_reduce(
                    out=stt_s[:, 0:1], in_=xsl(), axis=AXX, op=ALU.add), inc="V")
                s_sq = b.op("scalar", lambda e, xsl=xsl: e.activation(
                    out=sqf[:, :], in_=xsl(), func=ACT.Square),
                    deps=[("V", b.cnt["V"])], inc="SC")
                v_s2 = b.op("vector", lambda e: e.tensor_reduce(
                    out=stt_s[:, 1:2], in_=sqf[:, :], axis=AXX, op=ALU.add),
                    deps=[("SC", s_sq)], inc="V")
                b.op("vector", lambda e: e.tensor_scalar(
                    out=stt_s[:, 2:3], in0=stt_s[:, 0:1], scalar1=1.0 / D,
                    scalar2=None, op0=ALU.mult), inc="V")
                b.op("vector", lambda e: e.tensor_scalar(
                    out=stt_s[:, 3:4], in0=stt_s[:, 1:2], scalar1=1.0 / D,
                    scalar2=None, op0=ALU.mult), inc="V")
                b.op("vector", lambda e: e.tensor_tensor(
                    out=stt_s[:, 4:5], in0=stt_s[:, 2:3], in1=stt_s[:, 2:3],
                    op=ALU.mult), inc="V")
                b.op("vector", lambda e: e.scalar_tensor_tensor(
                    out=stt_s[:, 5:6], in0=stt_s[:, 3:4], scalar=LN_EPS,
                    in1=stt_s[:, 4:5], op0=ALU.add, op1=ALU.subtract), inc="V")
                s_sd = b.op("scalar", lambda e: e.activation(
                    out=stt_s[:, 6:7], in_=stt_s[:, 5:6], func=ACT.Sqrt),
                    deps=[("V", b.cnt["V"])], inc="SC")
                v_ri = b.op("vector", lambda e: e.reciprocal(
                    out=stt_s[:, 7:8], in_=stt_s[:, 6:7]), deps=[("SC", s_sd)], inc="V")
                hbuf = h2t[tc % 2]
                v_z = b.op("vector", lambda e, xsl=xsl, hbuf=hbuf: e.tensor_scalar(
                    out=hbuf[:, :], in0=xsl(), scalar1=stt_s[:, 2:3],
                    scalar2=stt_s[:, 7:8], op0=ALU.subtract, op1=ALU.mult),
                    deps=[("V", v_ri)] + list(h2_free[tc % 2]), inc="V")
                v_h1 = b.op("vector", lambda e, hbuf=hbuf: e.tensor_tensor(
                    out=hbuf[:, :], in0=hbuf[:, :], in1=cb1_tm[:, :], op=ALU.mult),
                    deps=[("V", v_z)], inc="V")
                v_h2 = b.op("vector", lambda e, hbuf=hbuf: e.tensor_tensor(
                    out=hbuf[:, :], in0=hbuf[:, :], in1=sb_tm[:, :], op=ALU.add),
                    deps=[("V", v_h1)], inc="V")
                if c.debug and tc == 0:
                    b.op("sync", lambda e, hbuf=hbuf: e.dma_start(
                        out=dbg["dbg_h2t0"][:, :], in_=hbuf[:, :]),
                        deps=[("V", v_h2)], inc="HO", inc_by=16)
                dqb = dqx[tc % 2]
                halves = []
                for hf in range(4):
                    dv, _ = emit_quant(hbuf[:, hf * 512:(hf + 1) * 512],
                                       dqb[:, hf * 512:(hf + 1) * 512], 512,
                                       [("V", v_h2), dq_free[tc % 2]])
                    halves.append(dv)
                h2_free[tc % 2] = [("V", max(halves))]
                if c.debug and tc == 0:
                    h2_free[0].append(("HO", b.cnt["HO"]))
                if c.debug and tc == 0:
                    b.need("sync", [("HO", b.cnt["HO"])])
                    b.op("sync", lambda e, dqb=dqb: e.dma_start(
                        out=dbg["dbg_dqx0"][:, :], in_=dqb[:, :]),
                        deps=[("V", max(halves))], inc="HO", inc_by=16)
                txs = "TX0" if tc % 2 == 0 else "TX1"
                vt = None
                exdep = ([("HO", b.cnt["HO"])] if (c.debug and tc == 0) else [])
                for hc in range(NK):
                    vt = b.op("sync", lambda e, tc=tc, hc=hc, dqb=dqb:
                              e.dma_start_transpose(
                                  out=x2f[:, (hc * NTC + tc) * 128:
                                          (hc * NTC + tc) * 128 + 128],
                                  in_=dqb[:, hc * 128:(hc + 1) * 128]),
                              deps=[("V", max(halves))] + exdep,
                              inc=txs, inc_by=16)
                    exdep = []
                dq_free[tc % 2] = (txs, vt)
            b.need("sync", [("TX0", b.cnt["TX0"]), ("TX1", b.cnt["TX1"])])
            if c.debug:
                b.op("sync", lambda e: e.dma_start(out=dbg["dbg_x2f"][:, :],
                     in_=x2f[:, :]), inc="HO", inc_by=16)
                b.need("sync", [("HO", b.cnt["HO"])])
            b.run_phase("p4_ln3")
        st_p4.close()
        x2f_deps = [("TX0", b.cnt["TX0"]), ("TX1", b.cnt["TX1"])]

        with contextlib.ExitStack() as ph:
            Rrm = [sb(ph, f"Rrm{i}", [128, 1024], F32) for i in range(2)]
            Ffm = Rrm
            amx = sb(ph, "amx", [128, 64], F32)
            sPd = [sb(ph, f"sPd{i}", [128, 128], BF16) for i in range(2)]
            sT_w = sb(ph, "sT_w", [128, 2 * TT], BF16)
            iT_w = sb(ph, "iT_w", [128, 2 * TT], BF16)
            sT_d = sb(ph, "sT_d", [128, D], BF16)
            iT_d = sb(ph, "iT_d", [128, D], BF16)
            gwf = sb(ph, "gwf", [128, NK * TT], BF16)
            upf = sb(ph, "upf", [128, NK * TT], BF16)
            dwf = sb(ph, "dwf", [128, NK * TT], BF16)
            hfq = [sb(ph, "hfq0", [128, NTC * 4 * 128], BF16)]
            htl = [sb(ph, "htl0", [128, TT], BF16)]
            hsl = [sb(ph, "hsl0", [128, TT], BF16)]
            dqh = [sb(ph, "dqh0", [128, TT], BF16)]


            b.op("gpsimd", lambda e: e.memset(sPd[0][:, :], 1.0), inc="Q")
            b.op("gpsimd", lambda e: e.memset(sPd[1][:, :], 1.0), inc="Q")
            v_qpd = b.cnt["Q"]
            Rrm_free = {0: [("Q", 0)], 1: [("Q", 0)]}
            Ffm_free = Rrm_free
            sPd_free = {0: ("Q", 0), 1: ("Q", 0)}
            dqh_free = {0: [("Q", 0)], 1: [("Q", 0)]}
            htl_free = {0: ("Q", 0), 1: ("Q", 0)}
            hsl_free = {0: ("Q", 0), 1: ("Q", 0)}
            gwf_free_P = 0
            upf_free_P = 0

            dwf_free_P = 0
            hfq_free_P = {0: 0, 1: 0}
            sT_free = {"w": ("Q", 0), "d": ("Q", 0)}
            hid = [0]
            rid = [0]
            fid = rid
            NQ = c.NQ
            inv_bank = [0]
            s_bank = [0]

            for qd in range(NQ):
                for wsel, wsrc in [(0, gw_r), (1, up_r)]:
                    for cc in range(4):
                        ic = qd * 4 + cc
                        v_am = 0
                        for half in range(2):
                            ri = rid[0]
                            rid[0] = 1 - ri
                            vld = b.op("sync", lambda e, wsrc=wsrc, ic=ic, ri=ri,
                                       half=half: e.dma_start(
                                           out=Rrm[ri][:, :],
                                           in_=wsrc[ic * 128:(ic + 1) * 128,
                                                    half * 1024:(half + 1) * 1024]),
                                       deps=Rrm_free[ri],
                                       inc="RL0" if ri == 0 else "RL1", inc_by=16)
                            v_am = b.op("vector", lambda e, ri=ri, half=half:
                                        e.tensor_reduce(
                                            out=amx[:, half * 32:half * 32 + 32],
                                            in_=Rrm[ri][:, :].rearrange(
                                                "p (b j) -> p b j", j=32),
                                            axis=AXX, op=ALU.max,
                                            apply_absolute_value=True),
                                        deps=[("RL0" if ri == 0 else "RL1", vld)],
                                        inc="V")
                            Rrm_free[ri] = [("V", v_am)]
                        si = cc % 2
                        v_s = b.op("vector", lambda e, si=si: e.tensor_scalar(
                            out=sPd[si][:, 0:64], in0=amx[:, 0:64],
                            scalar1=1.0 / 448.0, scalar2=1e-30,
                            op0=ALU.mult, op1=ALU.max),
                            deps=[sPd_free[si], ("Q", v_qpd)], inc="V")
                        vtr = b.op("scalar", lambda e, wsel=wsel, cc=cc, si=si:
                                   e.dma_start_transpose(
                                       out=sT_w[:, wsel * TT + cc * 128:
                                               wsel * TT + cc * 128 + 128],
                                       in_=sPd[si][:, :]),
                                   deps=[("V", v_s), sT_free["w"]],
                                   inc="TS0" if si == 0 else "TS1", inc_by=16)
                        sPd_free[si] = ("TS0" if si == 0 else "TS1", vtr)
                for hc in range(NK):
                    ri = rid[0]
                    rid[0] = 1 - ri
                    vld = b.op("sync", lambda e, hc=hc, qd=qd, ri=ri: e.dma_start(
                        out=Rrm[ri][:, 0:TT],
                        in_=dw_r[hc * 128:(hc + 1) * 128, qd * TT:(qd + 1) * TT]),
                        deps=Rrm_free[ri],
                        inc="RL0" if ri == 0 else "RL1", inc_by=16)
                    v_am = b.op("vector", lambda e, ri=ri: e.tensor_reduce(
                        out=amx[:, 0:16],
                        in_=Rrm[ri][:, 0:TT].rearrange("p (b j) -> p b j", j=32),
                        axis=AXX, op=ALU.max, apply_absolute_value=True),
                        deps=[("RL0" if ri == 0 else "RL1", vld)], inc="V")
                    Rrm_free[ri] = [("V", v_am)]
                    si = hc % 2
                    v_s = b.op("vector", lambda e, si=si: e.tensor_scalar(
                        out=sPd[si][:, 0:16], in0=amx[:, 0:16],
                        scalar1=1.0 / 448.0, scalar2=1e-30,
                        op0=ALU.mult, op1=ALU.max),
                        deps=[sPd_free[si], ("Q", v_qpd)], inc="V")
                    vtr = b.op("scalar", lambda e, hc=hc, si=si:
                               e.dma_start_transpose(
                                   out=sT_d[:, hc * 128:(hc + 1) * 128],
                                   in_=sPd[si][:, :]),
                               deps=[("V", v_s), sT_free["d"]],
                               inc="TS0" if si == 0 else "TS1", inc_by=16)
                    sPd_free[si] = ("TS0" if si == 0 else "TS1", vtr)
                pass
                def _ri(e, dst, src):
                    with nc.allow_low_precision(reason="bf16 scale reciprocal"):
                        return e.reciprocal(out=dst, in_=src)
                ts_tot = [("TS0", b.cnt["TS0"]), ("TS1", b.cnt["TS1"])]
                v_iw = b.op("vector", lambda e: _ri(e, iT_w[:, :], sT_w[:, :]),
                            deps=ts_tot + [sT_free["w"]], inc="V")
                v_id2 = b.op("vector", lambda e: _ri(e, iT_d[:, :], sT_d[:, :]),
                             deps=ts_tot + [sT_free["d"]], inc="V")
                v_id2 = b.op("vector", lambda e: e.tensor_tensor(
                    out=sT_d[:, :], in0=sT_d[:, :], in1=gb_tm[:, :],
                    op=ALU.mult), deps=[("V", v_id2)], inc="V")

                last_dq_w = 0
                for wsel, (wsrc, fdst, fp) in enumerate(
                        [(gwT_h, gwf, gwf_free_P), (upT_h, upf, upf_free_P)]):
                    for hc in range(NK):
                        fi = fid[0]
                        fid[0] = 1 - fi
                        vld = b.op("scalar", lambda e, wsrc=wsrc, hc=hc, qd=qd, fi=fi:
                                   e.dma_start(
                                       out=Ffm[fi][:, 0:TT],
                                       in_=wsrc[hc * 128:(hc + 1) * 128,
                                                qd * TT:(qd + 1) * TT]),
                                   deps=Ffm_free[fi],
                                   inc="FL0" if fi == 0 else "FL1", inc_by=16)
                        ib = 4 + inv_bank[0]
                        inv_bank[0] = 1 - inv_bank[0]
                        vmi = b.op("tensor", lambda e, ib=ib, hc=hc, wsel=wsel:
                                   e.matmul(ps[ib][:, 0:TT],
                                            REPall[:, hc * 128:(hc + 1) * 128],
                                            iT_w[0:64,
                                                 wsel * TT:(wsel + 1) * TT],
                                            start=True, stop=True),
                                   deps=[("V", max(v_iw, v_rep)), bank_free[ib]],
                                   inc="P")
                        bank_ready[ib] = vmi
                        sb_ = 6 + s_bank[0]
                        s_bank[0] = 1 - s_bank[0]
                        vms = b.op("tensor", lambda e, sb_=sb_, hc=hc, wsel=wsel:
                                   e.matmul(ps[sb_][:, 0:TT],
                                            REPall[:, hc * 128:(hc + 1) * 128],
                                            sT_w[0:64,
                                                 wsel * TT:(wsel + 1) * TT],
                                            start=True, stop=True),
                                   deps=[bank_free[sb_]], inc="P")
                        bank_ready[sb_] = vms
                        dv, fr = emit_quant_ps(
                            Ffm[fi][:, 0:TT],
                            fdst[:, hc * TT:(hc + 1) * TT], TT,
                            [("FL0" if fi == 0 else "FL1", vld),
                             ("P", max(vmi, vms)),
                             ("P", fp if hc == 0 else 0)],
                            ps[ib][:, 0:TT], ps[sb_][:, 0:TT])
                        Ffm_free[fi] = [fr[1]]
                        bank_free[ib] = fr[1]
                        bank_free[sb_] = ("V", dv)
                        last_dq_w = max(last_dq_w, dv)
                v_wq = last_dq_w
                last_dq_d = 0
                for icr in range(4):
                    for ht in range(4):
                        fi = fid[0]
                        fid[0] = 1 - fi
                        ic = qd * 4 + icr
                        vld = b.op("scalar", lambda e, ic=ic, ht=ht, fi=fi:
                                   e.dma_start(
                                       out=Ffm[fi][:, 0:TT],
                                       in_=dwT_h[ic * 128:(ic + 1) * 128,
                                                 ht * TT:(ht + 1) * TT]),
                                   deps=Ffm_free[fi],
                                   inc="FL0" if fi == 0 else "FL1", inc_by=16)
                        ib = 4 + inv_bank[0]
                        inv_bank[0] = 1 - inv_bank[0]
                        vmi = b.op("tensor", lambda e, ib=ib, icr=icr, ht=ht:
                                   e.matmul(ps[ib][:, 0:TT],
                                            REPall[:, icr * 128:(icr + 1) * 128],
                                            iT_d[0:64,
                                                 ht * TT:(ht + 1) * TT],
                                            start=True, stop=True),
                                   deps=[("V", max(v_id2, v_rep)), bank_free[ib]],
                                   inc="P")
                        bank_ready[ib] = vmi
                        sb_ = 6 + s_bank[0]
                        s_bank[0] = 1 - s_bank[0]
                        vms = b.op("tensor", lambda e, sb_=sb_, icr=icr, ht=ht:
                                   e.matmul(ps[sb_][:, 0:TT],
                                            REPall[:, icr * 128:(icr + 1) * 128],
                                            sT_d[0:64,
                                                 ht * TT:(ht + 1) * TT],
                                            start=True, stop=True),
                                   deps=[bank_free[sb_], ("V", v_id2)], inc="P")
                        bank_ready[sb_] = vms
                        dv, fr = emit_quant_ps(
                            Ffm[fi][:, 0:TT],
                            dwf[:, (icr * 4 + ht) * TT:(icr * 4 + ht) * TT + TT], TT,
                            [("FL0" if fi == 0 else "FL1", vld),
                             ("P", max(vmi, vms)),
                             ("P", dwf_free_P if (icr == 0 and ht == 0) else 0)],
                            ps[ib][:, 0:TT], ps[sb_][:, 0:TT])
                        Ffm_free[fi] = [fr[1]]
                        bank_free[ib] = fr[1]
                        bank_free[sb_] = ("V", dv)
                        last_dq_d = max(last_dq_d, dv)
                v_dq_d = last_dq_d
                sT_free["w"] = ("P", b.cnt["P"])
                sT_free["d"] = ("P", b.cnt["P"])

                hbuf = hfq[0]
                for tc in range(NTC):
                    bg, bu = 0, 1
                    mmdeps = [("V", v_wq)] + x2f_deps + [bank_free[bg]]
                    vg = None
                    for hc in range(NK):
                        vg = b.op("tensor", lambda e, tc=tc, hc=hc, bg=bg,
                                  first=(hc == 0), last=(hc == NK - 1):
                                  e.matmul(ps[bg][:, 0:TT],
                                           x2f[:, (hc * NTC + tc) * 128:
                                               (hc * NTC + tc) * 128 + 128],
                                           gwf[:, hc * TT:(hc + 1) * TT],
                                           start=first, stop=last),
                                  deps=mmdeps, inc="P")
                        mmdeps = []
                    mmdeps = [("V", v_wq), bank_free[bu]]
                    vu = None
                    for hc in range(NK):
                        vu = b.op("tensor", lambda e, tc=tc, hc=hc, bu=bu,
                                  first=(hc == 0), last=(hc == NK - 1):
                                  e.matmul(ps[bu][:, 0:TT],
                                           x2f[:, (hc * NTC + tc) * 128:
                                               (hc * NTC + tc) * 128 + 128],
                                           upf[:, hc * TT:(hc + 1) * TT],
                                           start=first, stop=last),
                                  deps=mmdeps, inc="P")
                        mmdeps = []
                    bank_ready[bg], bank_ready[bu] = vg, vu
                    if tc == NTC - 1:
                        gwf_free_P = vg
                        upf_free_P = vu
                    hi = 0
                    s_si = b.op("scalar", lambda e, bg=bg, hi=hi: e.activation(
                        out=hsl[hi][:, :], in_=ps[bg][:, 0:TT], func=ACT.Silu),
                        deps=[("P", vg), hsl_free[hi]], inc="SC")
                    bank_free[bg] = ("SC", s_si)
                    g_h = b.op("vector", lambda e, bu=bu, hi=hi: e.tensor_tensor(
                        out=htl[hi][:, :], in0=hsl[hi][:, :], in1=ps[bu][:, 0:TT],
                        op=ALU.mult), deps=[("SC", s_si), ("P", vu), htl_free[hi]],
                        inc="V")
                    bank_free[bu] = ("V", g_h)
                    hsl_free[hi] = ("V", g_h)
                    if c.debug and qd == 0 and tc == 0:
                        b.op("sync", lambda e, hi=hi: e.dma_start(
                            out=dbg["dbg_h0"][:, :], in_=htl[hi][:, :]),
                            deps=[("V", g_h)], inc="HO", inc_by=16)
                    dv, fr = emit_quant(htl[hi][:, :], dqh[hi][:, :], TT,
                                        [("V", g_h)] + list(dqh_free[hi]))
                    htl_free[hi] = fr[0] if not c.debug else ("HO", b.cnt["HO"])
                    if c.debug and qd == 0 and tc == 0:
                        b.need("sync", [("HO", b.cnt["HO"])])
                        b.op("sync", lambda e, hi=hi: e.dma_start(
                            out=dbg["dbg_hq0"][:, :], in_=dqh[hi][:, :]),
                            deps=[("V", dv)], inc="HO", inc_by=16)
                    ths = "TH0" if hi == 0 else "TH1"
                    vt = None
                    exdep = ([("HO", b.cnt["HO"])]
                             if (c.debug and qd == 0 and tc == 0) else [])
                    for ccx in range(4):
                        vt = b.op("sync", lambda e, tc=tc, hi=hi, hbuf=hbuf, ccx=ccx:
                                  e.dma_start_transpose(
                                      out=hbuf[:, (tc * 4 + ccx) * 128:
                                              (tc * 4 + ccx) * 128 + 128],
                                      in_=dqh[hi][:, ccx * 128:(ccx + 1) * 128]),
                                  deps=[("V", dv), ("P", hfq_free_P[0])] + exdep,
                                  inc=ths, inc_by=16)
                        exdep = []
                    dqh_free[hi] = [(ths, vt)]
                tr_hq = [("TH0", b.cnt["TH0"]), ("TH1", b.cnt["TH1"])]

                last_q = (qd == NQ - 1)
                vlast = None
                for tc in range(NTC):
                    for ht in range(4):
                        bank = 2 + ((tc * 4 + ht) % 2)
                        mmdeps = [("V", v_dq_d)] + tr_hq + [bank_free[bank]]
                        vd = None
                        for ccc in range(4):
                            vd = b.op("tensor", lambda e, tc=tc, ht=ht, cc=ccc,
                                      bank=bank, hbuf=hbuf,
                                      first=(ccc == 0), last=(ccc == 3):
                                      e.matmul(ps[bank][:, 0:TT],
                                               hbuf[:, (tc * 4 + cc) * 128:
                                                    (tc * 4 + cc) * 128 + 128],
                                               dwf[:, (cc * 4 + ht) * TT:
                                                   (cc * 4 + ht) * TT + TT],
                                               start=first, stop=last),
                                      deps=mmdeps, inc="P")
                            mmdeps = []
                        bank_ready[bank] = vd
                        vlast = vd
                        asl = lambda tc=tc, ht=ht: xmid[:, tc * D + ht * TT:
                                                        tc * D + ht * TT + TT]
                        g_ac = b.op("vector", lambda e, bank=bank, asl=asl:
                                    e.tensor_tensor(
                                        out=asl(), in0=asl(),
                                        in1=ps[bank][:, 0:TT], op=ALU.add),
                                    deps=[("P", vd)], inc="V")
                        bank_free[bank] = ("V", g_ac)
                        if last_q:
                            vho = b.op("sync", lambda e, tc=tc, ht=ht, asl=asl:
                                       e.dma_start(
                                           out=out_tm_h[tc * 128:(tc + 1) * 128,
                                                        ht * TT:(ht + 1) * TT],
                                           in_=asl()),
                                       deps=[("V", g_ac)], inc="HO0", inc_by=16)
                dwf_free_P = vlast
                hfq_free_P[0] = vlast
            b.need("sync", [("HO", b.cnt["HO"]), ("HO0", b.cnt["HO0"])])
            b.run_phase("p5_ffn")
        st_xm.close()
    return nc



def stage_inputs(c: Cfg, inputs):
    x = np.asarray(inputs["L0_attn_x"])
    D, T, S, NK, TO = c.D, c.T, c.S, c.NK, c.TO

    def pp(v):
        return np.ascontiguousarray(np.asarray(v).reshape(NK, 128).T)

    W_shift = np.asarray(inputs["L0_attn_W_shift"])
    wsh_pk = np.ascontiguousarray(
        W_shift.reshape(NK, 128, NK, 128).transpose(0, 3, 2, 1).reshape(NK * 128, NK * 128))
    W_key = np.asarray(inputs["L0_attn_W_key"])
    wk_pk = np.ascontiguousarray(
        W_key.reshape(S, NK, 128).transpose(2, 1, 0).reshape(128, NK * S))
    W_val = np.asarray(inputs["L0_attn_W_value"])
    wv_pk = np.ascontiguousarray(
        W_val.reshape(S, NK, 128).transpose(2, 1, 0).reshape(128, NK * S))

    shared = {
        "wsh_pk": wsh_pk,
        "wk_pk": wk_pk,
        "wv_pk": wv_pk,
        "woutT": np.ascontiguousarray(np.asarray(inputs["L0_attn_W_output"]).T),
        "gw_r": np.ascontiguousarray(np.asarray(inputs["L0_ffn_gate_w"])),
        "up_r": np.ascontiguousarray(np.asarray(inputs["L0_ffn_up_w"])),
        "dw_r": np.ascontiguousarray(np.asarray(inputs["L0_ffn_down_w"])),
        "gwT": np.ascontiguousarray(np.asarray(inputs["L0_ffn_gate_w"]).T),
        "upT": np.ascontiguousarray(np.asarray(inputs["L0_ffn_up_w"]).T),
        "dwT": np.ascontiguousarray(np.asarray(inputs["L0_ffn_down_w"]).T),
        "wmod": np.ascontiguousarray(np.asarray(inputs["L0_wmod"])),
        "gatepp": pp(inputs["L0_attn_gate"]),
        "lngpp": pp(inputs["L0_attn_ln_gamma"]),
        "lnbpp": pp(inputs["L0_attn_ln_beta"]),
        "td": np.ascontiguousarray(np.asarray(inputs["L0_attn_time_decay"]).reshape(S, 1)),
        "tf": np.ascontiguousarray(np.asarray(inputs["L0_attn_time_first"]).reshape(S, 1)),
    }
    cmat = np.asarray(inputs["L0_c"])
    in_maps = []
    for core in range(8):
        bi, h = core // 2, core % 2
        xb = np.roll(x[bi], -TO * h, axis=0)
        m = dict(shared)
        m["xT"] = np.ascontiguousarray(xb.T)
        m["x_tm"] = np.ascontiguousarray(xb[:TO])
        m["cpp"] = pp(cmat[bi])
        m["cmask"] = np.full((S, 1), float(h), np.float32)
        in_maps.append(m)
    return in_maps


_PROG_CACHE = {}


def kernel(**inputs):
    c = Cfg()
    if "prog" not in _PROG_CACHE:
        _PROG_CACHE["prog"] = build_program(c)
    nc = _PROG_CACHE["prog"]
    in_maps = stage_inputs(c, inputs)
    res = run_bass_kernel_spmd(nc, in_maps, core_ids=list(range(8)))
    out = np.empty((4, c.T, c.D), np.float32)
    for core in range(8):
        bi, h = core // 2, core % 2
        out[bi, h * c.TO:(h + 1) * c.TO, :] = res.results[core]["out_tm"]
    return out
